# version 19
# baseline (speedup 1.0000x reference)
import os
import numpy as np
from contextlib import ExitStack
import concourse.bass as bass
import concourse.mybir as mybir
from concourse.bass_utils import run_bass_kernel_spmd

F32 = mybir.dt.float32
BF16 = mybir.dt.bfloat16
AF = mybir.ActivationFunctionType
OP = mybir.AluOpType
AX = mybir.AxisListType

MAGIC = 3 * 2.0 ** 22
D = 1024
S = 2048
NCH = 16
DH = 4096
NCJ = 32
EPS = 1e-5

A_N1T = 0
A_HS = 16384
A_W1S = 49152
A_W2S = 53248
A_S2BC = 57344
A_M = 58368
A_S2R = 58368
A_TOT = 59392
AT_VG = 16384
AT_Q = 33024
AT_K = 35072
AT_PT = 37120
AT_WQS = 39168
AT_WVS = 43264
PT_SLOT = (37120, 38144, 49152, 50176)
P_STG = 0
P_TERN = 32768

F_HRAW = (0, 1026)
F_CT = 2052
F_BN = 3076
F_MV = 3088
F_MX = 3120
F_MN = 3136
F_R = 3152
F_MRB = 3168
F_SA = 3184
F_BA = 3200
F_IS1 = 3216
F_TA = 3232
F_TB = 3248
F_MX2 = 3264
F_S2 = 3272
F_S2P = 3280
F_ESC = 3288
F_RV = 3296
F_SQ = 3300
F_VP = 3316
F_W = 3332
F_SCQ = 3364
F_ONE = 3372
F_WT = 3392
F_BNB = 3392
F_CTXS = 4416
F_CTXA = 3584
F_XQT = 3392
F_TOT = 5440

ENGS = ("sync", "tensor", "vector", "scalar", "gpsimd")
SIM_SERIAL = os.environ.get("BASS_SIM_FP64") == "1"


class Sem:
    __slots__ = ("h", "v", "name")

    def __init__(self, nc, name):
        self.h = nc.alloc_semaphore(name=name)
        self.v = 0
        self.name = name


class Prog:
    def __init__(self, nc):
        self.nc = nc
        self.ops = {e: [] for e in ENGS}
        self.es = {e: Sem(nc, "es_" + e) for e in ENGS}
        self._lastwait = {}

    def do(self, e, fn, inc=True):
        if SIM_SERIAL:
            s = self.es[e]
            s.v += 1
            self.ops[e].append(lambda eng, f=fn, h=s.h: f(eng).then_inc(h, 1))
            self.ops[e].append(lambda eng, h=s.h, v=s.v: eng.wait_ge(h, v))
            self._lastwait[(e, s.name)] = s.v
            return s.v
        if inc:
            s = self.es[e]
            s.v += 1
            self.ops[e].append(lambda eng, f=fn, h=s.h: f(eng).then_inc(h, 1))
            return s.v
        self.ops[e].append(fn)
        return None

    def dma(self, e, out, in_, dsem, slow=False):
        dsem.v += 16
        self.ops[e].append(
            lambda eng, o=out, i=in_, h=dsem.h, sl=slow: eng.dma_start(
                out=o, in_=i, allow_slow_non_contiguous=sl).then_inc(h, 16))
        return dsem.v

    def wait(self, e, sem, val):
        if not val:
            return
        key = (e, sem.name)
        if self._lastwait.get(key, -1) >= val:
            return
        self._lastwait[key] = val
        self.ops[e].append(lambda eng, h=sem.h, v=val: eng.wait_ge(h, v))

    def selfwait(self, e):
        self.wait(e, self.es[e], self.es[e].v)

    def barrier(self, dma_sems=()):
        snap = {e: self.es[e].v for e in ENGS}
        for e in ENGS:
            for o in ENGS:
                if o != e:
                    self.wait(e, self.es[o], snap[o])
            for ds in dma_sems:
                self.wait(e, ds, ds.v)


def r2(t, lo, n, inner):
    return t[:, lo:lo + n].rearrange("p (a b) -> p a b", b=inner)


def build_program(n_steps=6, dump=None, ffn_stop=None):
    nc = bass.Bass("TRN2", target_bir_lowering=False)
    es = ExitStack()
    nc._keepalive_es = es

    xin = nc.dram_tensor("xin", [2, S, D], F32, kind="ExternalInput")
    wq_d = nc.dram_tensor("wq", [D, D], F32, kind="ExternalInput")
    wk_d = nc.dram_tensor("wk", [D, D], F32, kind="ExternalInput")
    wv_d = nc.dram_tensor("wv", [D, D], F32, kind="ExternalInput")
    bq_d = nc.dram_tensor("bq", [D], F32, kind="ExternalInput")
    bk_d = nc.dram_tensor("bk", [D], F32, kind="ExternalInput")
    bv_d = nc.dram_tensor("bv", [D], F32, kind="ExternalInput")
    w1_d = nc.dram_tensor("w1", [DH, D], F32, kind="ExternalInput")
    w2_d = nc.dram_tensor("w2", [D, DH], F32, kind="ExternalInput")
    cw_d = nc.dram_tensor("cw", [DH, 1, 3], F32, kind="ExternalInput")
    cb_d = nc.dram_tensor("cb", [DH], F32, kind="ExternalInput")
    xout = nc.dram_tensor("xout", [2, S, D], F32, kind="ExternalOutput")
    scr_w1 = nc.dram_tensor("scr_w1", [128, 32768], BF16)
    scr_w2 = nc.dram_tensor("scr_w2", [128, 32768], BF16)
    scr_wq = nc.dram_tensor("scr_wq", [128, 8192], BF16)
    scr_wk = nc.dram_tensor("scr_wk", [128, 8192], BF16)
    scr_wv = nc.dram_tensor("scr_wv", [128, 8192], BF16)

    def sb(name, shape, dt):
        return es.enter_context(nc.sbuf_tensor(name, shape, dt))

    def ps(name, shape, dt):
        return es.enter_context(nc.psum_tensor(name, shape, dt))

    x_sb = sb("x_sb", [128, NCH * D], F32)
    A = sb("A", [128, A_TOT], BF16)
    F = sb("F", [128, F_TOT], F32)
    idf = sb("idf", [128, 128], F32)
    onesb = sb("onesb", [1, 128], BF16)
    onesf = sb("onesf", [1, 128], F32)
    cwp = sb("cwp", [128, 96], F32)
    cbp = sb("cbp", [128, 32], F32)
    bqs = sb("bqs", [128, 8], F32)
    bks = sb("bks", [128, 8], F32)
    scq = sb("scq", [128, 8], F32)
    sck = sb("sck", [128, 8], F32)
    bvb = sb("bvb", [1, D], BF16)
    sint = sb("sint", [128, 512], mybir.dt.int32)
    mwb = sb("mwb", [128, 4], F32)

    PS_A = ps("PS_A", [128, 1536], F32)
    PS_B = ps("PS_B", [128, 1536], F32)
    PS_C = ps("PS_C", [128, 1024], F32)

    P = Prog(nc)
    q_x = [Sem(nc, "q_x0"), Sem(nc, "q_x1")]
    q_out = [Sem(nc, "q_o0"), Sem(nc, "q_o1")]
    q_pw = [Sem(nc, "q_pw0"), Sem(nc, "q_pw1")]
    q_scr = Sem(nc, "q_scr")
    q_sm = Sem(nc, "q_sm")
    q_bv = Sem(nc, "q_bv")
    q_w1 = [Sem(nc, "q_w1a"), Sem(nc, "q_w1b")]
    q_w2 = [Sem(nc, "q_w2a"), Sem(nc, "q_w2b")]
    q_qk = [Sem(nc, "q_qka"), Sem(nc, "q_qkb")]
    q_wv = Sem(nc, "q_wv")

    tS, vS, sS, gS = P.es["tensor"], P.es["vector"], P.es["scalar"], P.es["gpsimd"]


    P.do("gpsimd", lambda g: g.iota(F[:, 0:128], [[1, 128]], channel_multiplier=0,
                                    allow_small_or_imprecise_dtypes=True))
    P.do("gpsimd", lambda g: g.iota(F[:, 128:256], [[0, 128]], channel_multiplier=1,
                                    allow_small_or_imprecise_dtypes=True))
    P.wait("vector", gS, gS.v)
    P.do("vector", lambda v: v.tensor_tensor(idf[:, :], F[:, 0:128], F[:, 128:256], OP.is_equal))
    P.do("vector", lambda v: v.memset(onesb[:, :], 1.0))
    P.do("vector", lambda v: v.memset(onesf[:, :], 1.0))
    P.do("vector", lambda v: v.memset(F[:, F_ONE:F_ONE + 1], 1.0))

    P.dma("sync", bqs[:, :], bass.AP(bq_d, 0, [[1, 128], [128, 8]]), q_sm, slow=True)
    P.dma("sync", bks[:, :], bass.AP(bk_d, 0, [[1, 128], [128, 8]]), q_sm, slow=True)
    P.dma("gpsimd", bvb[:, :], bass.AP(bv_d, 0, [[1024, 1], [1, 1024]]), q_bv)
    P.dma("sync", bass.AP(cwp, 0, [[96, 128], [32, 3], [1, 32]]),
          bass.AP(cw_d, 0, [[3, 128], [1, 3], [384, 32]]), q_sm, slow=True)
    P.dma("sync", cbp[:, :], bass.AP(cb_d, 0, [[1, 128], [128, 32]]), q_sm, slow=True)

    def w1_src(j):
        return bass.AP(w1_d, j * 128 * 1024, [[1024, 128], [1, 1024]])

    def w2_src(piece):
        i, qq = piece // 4, piece % 4
        return bass.AP(w2_d, i * 128 * 4096 + 1024 * qq, [[4096, 128], [1, 1024]])

    def wq_src(j):
        return bass.AP(wq_d, j * 128 * 1024, [[1024, 128], [1, 1024]])

    def wk_src(j):
        return bass.AP(wk_d, j * 128 * 1024, [[1024, 128], [1, 1024]])

    def wv_src(j):
        return bass.AP(wv_d, j * 128 * 1024, [[1024, 128], [1, 1024]])

    PREP_ST = F_MX
    p1 = [(w1_src, 32, OP.add, F_W),
          (w2_src, 32, OP.add, PREP_ST),
          (wq_src, 8, OP.max, PREP_ST + 32),
          (wk_src, 8, OP.max, PREP_ST + 40),
          (wv_src, 8, OP.max, PREP_ST + 48)]
    d1 = {}
    gi_ = 0
    p1_slots = [F_WT, F_WT + 1024, F_HRAW[0], F_HRAW[1]]
    q_p1 = [q_pw[0], q_pw[1], q_w1[0], q_w1[1]]
    P.wait("sync", vS, vS.v)
    for srcf, np_, op_, base in p1:
        for j in range(np_):
            par = gi_ % 4
            wt = F[:, p1_slots[par]:p1_slots[par] + 1024]
            P.wait("sync", vS, d1.get(gi_ - 4, 0))
            P.dma("sync", wt, srcf(j), q_p1[par])
            P.wait("vector", q_p1[par], q_p1[par].v)
            d1[gi_] = P.do("vector", lambda v, wt=wt, base=base, j=j, op_=op_: v.tensor_reduce(
                F[:, base + j:base + j + 1], wt, AX.X, op_, apply_absolute_value=True))
            gi_ += 1
    pass1_v_end = vS.v

    def mean_finalize(sum_base, npieces, mw_col, sw_col):
        P.selfwait("vector")
        P.do("vector", lambda v: v.tensor_reduce(
            F[:, F_TA:F_TA + 1], F[:, sum_base:sum_base + npieces], AX.X, OP.add))
        P.selfwait("vector")
        P.wait("tensor", vS, vS.v)
        P.do("tensor", lambda t: t.matmul(PS_C[0:1, 0:1], F[:, F_TA:F_TA + 1],
                                          F[:, F_ONE:F_ONE + 1], start=True, stop=True))
        P.wait("scalar", tS, tS.v)
        P.do("scalar", lambda s: s.activation(F[0:1, F_TA:F_TA + 1], PS_C[0:1, 0:1],
                                              AF.Identity, scale=1.0 / (4096.0 * 1024.0)))
        P.wait("vector", sS, sS.v)
        P.do("vector", lambda v: v.tensor_scalar_max(
            F[0:1, F_TA:F_TA + 1], F[0:1, F_TA:F_TA + 1], 1e-5))
        P.selfwait("vector")
        P.do("vector", lambda v: v.reciprocal(F[0:1, F_TB:F_TB + 1], F[0:1, F_TA:F_TA + 1]))
        P.selfwait("vector")
        P.wait("tensor", vS, vS.v)
        P.do("tensor", lambda t: t.matmul(PS_C[:, 4:5], onesf[0:1, :], F[0:1, F_TA:F_TA + 1],
                                          start=True, stop=True))
        P.do("tensor", lambda t: t.matmul(PS_C[:, 5:6], onesf[0:1, :], F[0:1, F_TB:F_TB + 1],
                                          start=True, stop=True))
        P.wait("scalar", tS, tS.v)
        P.do("scalar", lambda s: s.activation(mwb[:, mw_col:mw_col + 1], PS_C[:, 4:5], AF.Identity,
                                              scale=(0.5 if mw_col == 1 else 1.0)))
        P.do("scalar", lambda s: s.activation(mwb[:, sw_col:sw_col + 1], PS_C[:, 5:6], AF.Identity))
        P.wait("vector", sS, sS.v)

    mean_finalize(F_W, 32, 0, 2)
    mean_finalize(PREP_ST, 32, 1, 3)
    ATT_INV = PREP_ST + 56
    for idx, (mx_base, sc_dst) in enumerate(((PREP_ST + 32, scq), (PREP_ST + 40, sck),
                                             (PREP_ST + 48, None))):
        P.do("vector", lambda v, mx_base=mx_base, idx=idx: v.tensor_scalar(
            F[:, F_TA:F_TA + 8], F[:, mx_base:mx_base + 8], 1.0 / 127.0, 1e-8,
            OP.mult, OP.max))
        P.selfwait("vector")
        if sc_dst is not None:
            P.do("vector", lambda v, sc_dst=sc_dst: v.tensor_copy(sc_dst[:, :], F[:, F_TA:F_TA + 8]))
        else:
            P.do("vector", lambda v: v.tensor_copy(F[:, ATT_INV + 24:ATT_INV + 32],
                                                   F[:, F_TA:F_TA + 8]))
        P.do("vector", lambda v, idx=idx: v.reciprocal(
            F[:, ATT_INV + 8 * idx:ATT_INV + 8 * idx + 8], F[:, F_TA:F_TA + 8]))
        P.selfwait("vector")

    p2_first = [True]
    d2v, d2s, d2t = {}, {}, {}
    g2 = [0]

    def p2_generic(srcf, npieces, quant_fn, evac_dst, scr, scr_cols, stage_cols):
        if p2_first[0]:
            P.wait("sync", vS, pass1_v_end)
            p2_first[0] = False
        P.wait("scalar", q_scr, q_scr.v)
        for j in range(npieces):
            gj = g2[0]
            g2[0] += 1
            par = gj % 2
            wt = F[:, F_WT + 1024 * par:F_WT + 1024 * par + 1024]
            P.wait("sync", vS, d2v.get(gj - 2, 0))
            P.dma("sync", wt, srcf(j), q_pw[par])
            P.wait("vector", q_pw[par], q_pw[par].v)
            tern = F[:, F_HRAW[par]:F_HRAW[par] + 1024]
            P.wait("vector", tS, d2t.get(gj - 2, 0))
            d2v[gj] = quant_fn(j, wt, tern)
            PB = PS_C if par == 0 else PS_A
            P.wait("tensor", vS, d2v[gj])
            P.wait("tensor", sS, d2s.get(gj - 2, 0))
            tl = None
            for k in range(8):
                tl = P.do("tensor", lambda t_, k=k, PB=PB, tern=tern: t_.transpose(
                    PB[:, 128 * k:128 * k + 128], tern[:, 128 * k:128 * k + 128], idf[:, :]),
                    inc=(k == 7))
            d2t[gj] = tl
            P.wait("scalar", tS, tl)
            d2s[gj] = P.do("scalar", lambda s, j=j, PB=PB: s.activation(
                evac_dst(j), PB[:, 0:1024], AF.Identity))
        P.wait("gpsimd", sS, sS.v)
        P.dma("gpsimd", scr[:, 0:scr_cols], A[:, P_STG:P_STG + scr_cols], q_scr)

    def tern_quant(sw_col):
        def fn(j, wt, tern):
            tf = F[:, F_CT:F_CT + 1024]
            P.do("vector", lambda v, wt=wt, tf=tf: v.tensor_scalar(
                tf, wt, mwb[:, sw_col:sw_col + 1], None, OP.mult), inc=False)
            last = None
            for h in range(2):
                P.do("vector", lambda v, tf=tf, h=h: v.tensor_copy(
                    sint[:, :], tf[:, 512 * h:512 * h + 512]), inc=False)
                P.do("vector", lambda v: v.tensor_scalar(
                    sint[:, :], sint[:, :], 1, -1, OP.min, OP.max), inc=False)
                last = P.do("vector", lambda v, tern=tern, h=h: v.tensor_copy(
                    tern[:, 512 * h:512 * h + 512], sint[:, :]))
            return last
        return fn

    def attn_quant(inv_off, is_v):
        def fn(j, wt, tern):
            tf = F[:, F_CT:F_CT + 1024]
            P.do("vector", lambda v, wt=wt, j=j, tf=tf: v.tensor_scalar(
                tf, wt, F[:, ATT_INV + inv_off + j:ATT_INV + inv_off + j + 1], None,
                OP.mult), inc=False)
            last = None
            for h in range(2):
                P.do("vector", lambda v, tf=tf, h=h: v.tensor_copy(
                    sint[:, :], tf[:, 512 * h:512 * h + 512]), inc=False)
                if is_v:
                    last = P.do("vector", lambda v, tern=tern, h=h, j=j: v.tensor_scalar(
                        tern[:, 512 * h:512 * h + 512], sint[:, :],
                        F[:, ATT_INV + 24 + j:ATT_INV + 24 + j + 1], None, OP.mult))
                else:
                    last = P.do("vector", lambda v, tern=tern, h=h: v.tensor_copy(
                        tern[:, 512 * h:512 * h + 512], sint[:, :]))
            return last
        return fn

    def w2_dst(piece):
        i, qq = piece // 4, piece % 4
        h2, im = i // 4, i % 4
        off = 16384 * h2 + 512 * 8 * qq + 128 * im
        return bass.AP(A, P_STG + off, [[A_TOT, 128], [512, 8], [1, 128]])

    def wv_dst(j):
        return bass.AP(A, P_STG + 128 * j, [[A_TOT, 128], [1024, 8], [1, 128]])

    p2_generic(w1_src, 32, tern_quant(2),
               lambda j: A[:, P_STG + 1024 * j:P_STG + 1024 * j + 1024], scr_w1, 32768, None)
    p2_generic(w2_src, 32, tern_quant(3), w2_dst, scr_w2, 32768, None)
    p2_generic(wq_src, 8, attn_quant(0, False),
               lambda j: A[:, P_STG + 1024 * j:P_STG + 1024 * j + 1024], scr_wq, 8192, None)
    p2_generic(wk_src, 8, attn_quant(8, False),
               lambda j: A[:, P_STG + 1024 * j:P_STG + 1024 * j + 1024], scr_wk, 8192, None)
    p2_generic(wv_src, 8, attn_quant(16, True), wv_dst, scr_wv, 8192, None)

    P.wait("vector", q_sm, q_sm.v)
    P.do("vector", lambda v: v.tensor_scalar(cwp[:, :], cwp[:, :], mwb[:, 0:1], None, OP.mult))
    P.barrier(dma_sems=(q_scr, q_sm, q_bv))
    def dov(fn):
        P.do("vector", fn)
        P.selfwait("vector")

    def ln_stats_chunks(for_ffn, lo, hi, blk_base):
        for c in range(lo, hi):
            xc = x_sb[:, 1024 * c:1024 * c + 1024]
            off = blk_base + 12 * (c - lo) if blk_base is not None else F_BNB + 12 * c
            P.do("vector", lambda v, xc=xc, off=off: v.bn_stats(
                F[:, off:off + 6], xc[:, 0:512]), inc=False)
            last = (c == hi - 1) and not for_ffn
            P.do("vector", lambda v, xc=xc, off=off: v.bn_stats(
                F[:, off + 6:off + 12], xc[:, 512:1024]), inc=last)
            if for_ffn:
                P.do("vector", lambda v, xc=xc, c=c: v.tensor_reduce(
                    F[:, F_MX + c:F_MX + c + 1], xc, AX.X, OP.max), inc=False)
                P.do("vector", lambda v, xc=xc, c=c: v.tensor_reduce(
                    F[:, F_MN + c:F_MN + c + 1], xc, AX.X, OP.min), inc=(c == hi - 1))

    def ln_stats(for_ffn, pre_done=0):
        ln_stats_chunks(for_ffn, pre_done, NCH, None)
        P.selfwait("vector")
        for c in range(NCH):
            base = F_MX + 12 * c if c < pre_done else F_BNB + 12 * c
            P.do("vector", lambda v, c=c, base=base: v.bn_aggr(
                F[:, F_MV + 2 * c:F_MV + 2 * c + 2], F[:, base:base + 12]),
                 inc=(c == NCH - 1))
        P.selfwait("vector")
        mv = bass.AP(F, F_MV, [[F_TOT, 128], [2, 16]])
        vv_ = bass.AP(F, F_MV + 1, [[F_TOT, 128], [2, 16]])
        dov(lambda v: v.tensor_scalar(F[:, F_VP:F_VP + 16], vv_, EPS, None, OP.add))
        P.wait("scalar", vS, vS.v)
        P.do("scalar", lambda s: s.activation(F[:, F_SQ:F_SQ + 16], F[:, F_VP:F_VP + 16], AF.Sqrt))
        P.wait("vector", sS, sS.v)
        dov(lambda v: v.reciprocal(F[:, F_R:F_R + 16], F[:, F_SQ:F_SQ + 16]))
        dov(lambda v: v.tensor_tensor(F[:, F_TA:F_TA + 16], mv, F[:, F_R:F_R + 16], OP.mult))
        dov(lambda v: v.tensor_scalar(F[:, F_MRB:F_MRB + 16], F[:, F_TA:F_TA + 16],
                                      -1.0, None, OP.mult))
        if for_ffn:
            dov(lambda v: v.tensor_tensor(F[:, F_TA:F_TA + 16],
                                          F[:, F_MX:F_MX + 16], mv, OP.subtract))
            dov(lambda v: v.tensor_tensor(F[:, F_TB:F_TB + 16], mv,
                                          F[:, F_MN:F_MN + 16], OP.subtract))
            dov(lambda v: v.tensor_tensor(F[:, F_TA:F_TA + 16], F[:, F_TA:F_TA + 16],
                                          F[:, F_TB:F_TB + 16], OP.max))
            dov(lambda v: v.tensor_tensor(F[:, F_TA:F_TA + 16], F[:, F_TA:F_TA + 16],
                                          F[:, F_R:F_R + 16], OP.mult))
            dov(lambda v: v.tensor_scalar_max(F[:, F_TA:F_TA + 16], F[:, F_TA:F_TA + 16], 1e-5))
            dov(lambda v: v.tensor_scalar(F[:, F_IS1:F_IS1 + 16], F[:, F_TA:F_TA + 16],
                                          1.0 / 127.0, None, OP.mult))
            dov(lambda v: v.reciprocal(F[:, F_TB:F_TB + 16], F[:, F_TA:F_TA + 16]))
            dov(lambda v: v.tensor_scalar(F[:, F_TB:F_TB + 16], F[:, F_TB:F_TB + 16],
                                          127.0, None, OP.mult))
            dov(lambda v: v.tensor_tensor(F[:, F_SA:F_SA + 16], F[:, F_R:F_R + 16],
                                          F[:, F_TB:F_TB + 16], OP.mult))
            dov(lambda v: v.tensor_tensor(F[:, F_BA:F_BA + 16], F[:, F_MRB:F_MRB + 16],
                                          F[:, F_TB:F_TB + 16], OP.mult))
        P.selfwait("vector")

    def make_xhatT(for_ffn, interleave=None, v_hook=None, v_sdone=None):
        sdone = {}
        tdone = {}
        emitted = []

        def emit_chunk(c):
            xc = x_sb[:, 1024 * c:1024 * c + 1024]
            late = for_ffn and c > 8
            xq = F[:, F_XQT:F_XQT + 1024] if late else F[:, F_HRAW[c % 2]:F_HRAW[c % 2] + 1024]
            n1b = F[:, F_CT:F_CT + 1024]
            P.wait("vector", tS, tdone.get(c - 1 if late else c - 2, 0))
            if for_ffn:
                P.do("vector", lambda v, xc=xc, c=c, n1b=n1b: v.tensor_scalar(
                    n1b, xc, F[:, F_SA + c:F_SA + c + 1],
                    F[:, F_BA + c:F_BA + c + 1], OP.mult, OP.add), inc=False)
                t = None
                for h in range(2):
                    P.do("vector", lambda v, n1b=n1b, h=h: v.tensor_copy(
                        sint[:, :], n1b[:, 512 * h:512 * h + 512]), inc=False)
                    t = P.do("vector", lambda v, xq=xq, h=h, c=c: v.tensor_scalar(
                        xq[:, 512 * h:512 * h + 512], sint[:, :],
                        F[:, F_IS1 + c:F_IS1 + c + 1], None, OP.mult))
            else:
                t = P.do("vector", lambda v, xc=xc, c=c, xq=xq: v.tensor_scalar(
                    xq, xc, F[:, F_R + c:F_R + c + 1], F[:, F_MRB + c:F_MRB + c + 1],
                    OP.mult, OP.add))
            PB = PS_C if (late or c % 2 == 0) else PS_A
            P.wait("tensor", vS, t)
            P.wait("tensor", sS, sdone.get(c - 1 if late else c - 2, 0))
            tl = None
            for k in range(8):
                tl = P.do("tensor", lambda t_, k=k, xq=xq, PB=PB: t_.transpose(
                    PB[:, 128 * k:128 * k + 128], xq[:, 128 * k:128 * k + 128], idf[:, :]),
                    inc=(k == 7))
            tdone[c] = tl
            P.wait("scalar", tS, tl)
            dst = bass.AP(A, A_N1T + 128 * c, [[A_TOT, 128], [2048, 8], [1, 128]])
            sdone[c] = P.do("scalar", lambda s, dst=dst, PB=PB: s.activation(
                dst, PB[:, 0:1024], AF.Identity))

        for c in range(NCH):
            emit_chunk(c)
            if v_hook is not None:
                v_sdone.update(sdone)
                if c >= 2:
                    v_hook(c - 2)
                if c == NCH - 1:
                    v_hook(c - 1)
                    v_hook(c)
        return sdone

    w1_guard = [0]
    w2_guard = [0]

    def ffn_stage(seq, mid_hook=None):
        marks = []
        pre_sS, pre_vS = sS.v, vS.v
        ln_stats(for_ffn=True)
        xh_done = make_xhatT(for_ffn=True)
        for sblk in range(2):
            win_lo = 0 if sblk == 0 else 1023
            pad_col = 0 if sblk == 0 else 1025
            evac_off = 1 if sblk == 0 else 0
            P.wait("vector", sS, xh_done[8])
            P.do("vector", lambda v, pc=pad_col: v.memset(
                F[:, F_HRAW[0] + pc:F_HRAW[0] + pc + 1], 0.0), inc=False)
            P.do("vector", lambda v, pc=pad_col: v.memset(
                F[:, F_HRAW[1] + pc:F_HRAW[1] + pc + 1], 0.0))
            P.selfwait("vector")
            P.wait("tensor", sS, pre_sS if sblk == 0 else sS.v)
            P.wait("tensor", sS, xh_done[8 if sblk == 0 else NCH - 1])
            P.wait("tensor", vS, pre_vS if sblk == 0 else vS.v)
            mm_done, evac_done, conv_done, silu_done = {}, {}, {}, {}
            w1fill = {}
            widths = (512, 512, 1)
            offs = (0, 512, 1024)
            P.wait("sync", tS, w1_guard[0])
            for g in (0, 1):
                w1fill[g] = P.dma("sync", A[:, A_W1S + 2048 * (g % 2):A_W1S + 2048 * (g % 2) + 2048],
                                  scr_w1[:, 2048 * g:2048 * g + 2048], q_w1[g % 2])
            for cj in range(NCJ):
                g, gi = cj // 2, cj % 2
                par = g % 2
                if gi == 0 and g >= 1 and g + 1 < 16:
                    P.wait("sync", tS, mm_done[2 * (g - 1) + 1])
                    w1fill[g + 1] = P.dma(
                        "sync", A[:, A_W1S + 2048 * ((g + 1) % 2):A_W1S + 2048 * ((g + 1) % 2) + 2048],
                        scr_w1[:, 2048 * (g + 1):2048 * (g + 1) + 2048], q_w1[(g + 1) % 2])
                w1sl = A[:, A_W1S + 2048 * par + 1024 * gi:A_W1S + 2048 * par + 1024 * gi + 1024]
                PS = PS_A if cj % 2 == 0 else PS_B
                subs = (PS[:, 0:512], PS[:, 512:1024], PS[:, 1024:1025])
                P.wait("tensor", q_w1[par], w1fill[g])
                P.wait("tensor", sS, evac_done.get(cj - 2, 0))
                last = None
                for k in range(8):
                    stat = w1sl[:, 128 * k:128 * k + 128]
                    for si in range(3):
                        mv_ap = A[:, A_N1T + 2048 * k + win_lo + offs[si]:
                                  A_N1T + 2048 * k + win_lo + offs[si] + widths[si]]
                        last = P.do("tensor", lambda t_, sub=subs[si], stat=stat, mv=mv_ap,
                                    st=(k == 0), sp=(k == 7): t_.matmul(
                            sub, stat, mv, start=st, stop=sp),
                            inc=(k == 7 and si == 2))
                mm_done[cj] = last
                if cj == NCJ - 1:
                    w1_guard[0] = last
                hb = F_HRAW[cj % 2]
                P.wait("scalar", tS, last)
                P.wait("scalar", vS, conv_done.get(cj - 2, 0))
                ed = P.do("scalar", lambda s, PS=PS, hb=hb, o=evac_off: s.activation(
                    F[:, hb + o:hb + o + 1025], PS[:, 0:1025], AF.Identity))
                evac_done[cj] = ed
                ct = F[:, F_CT:F_CT + 512 * (cj % 2) + 0:] if False else \
                     (F[:, F_CT:F_CT + 1024] if cj % 2 == 0 else F[:, F_XQT:F_XQT + 1024])
                P.wait("vector", sS, ed)
                P.do("vector", lambda v, hb=hb, cj=cj, ct=ct: v.tensor_scalar(
                    ct, F[:, hb:hb + 1024], cwp[:, cj:cj + 1], None, OP.mult), inc=False)
                P.do("vector", lambda v, hb=hb, cj=cj, ct=ct: v.scalar_tensor_tensor(
                    ct, F[:, hb + 1:hb + 1025], cwp[:, 32 + cj:32 + cj + 1], ct,
                    OP.mult, OP.add), inc=False)
                conv_done[cj] = P.do("vector", lambda v, hb=hb, cj=cj, ct=ct: v.scalar_tensor_tensor(
                    ct, F[:, hb + 2:hb + 1026], cwp[:, 64 + cj:64 + cj + 1], ct,
                    OP.mult, OP.add))
                def emit_silu(pj):
                    P.wait("scalar", vS, conv_done[pj])
                    hs = A[:, A_HS + 1024 * pj:A_HS + 1024 * pj + 1024]
                    cts = F[:, F_CT:F_CT + 1024] if pj % 2 == 0 else F[:, F_XQT:F_XQT + 1024]
                    silu_done[pj] = P.do("scalar", lambda s, hs=hs, pj=pj, cts=cts: s.activation(
                        hs, cts, AF.Silu, bias=cbp[:, pj:pj + 1]))
                if cj >= 1:
                    emit_silu(cj - 1)
                if cj >= 2:
                    pj = cj - 2
                    P.wait("vector", sS, silu_done[pj])
                    if pj == 0:
                        P.do("vector", lambda v: v.tensor_copy(
                            A[:, A_M:A_M + 1024], A[:, A_HS:A_HS + 1024]))
                    else:
                        P.do("vector", lambda v, pj=pj: v.tensor_tensor(
                            A[:, A_M:A_M + 1024], A[:, A_M:A_M + 1024],
                            A[:, A_HS + 1024 * pj:A_HS + 1024 * pj + 1024], OP.max))
            emit_silu(NCJ - 1)
            for pj in (NCJ - 2, NCJ - 1):
                P.wait("vector", sS, silu_done[pj])
                P.do("vector", lambda v, pj=pj: v.tensor_tensor(
                    A[:, A_M:A_M + 1024], A[:, A_M:A_M + 1024],
                    A[:, A_HS + 1024 * pj:A_HS + 1024 * pj + 1024], OP.max))
            P.selfwait("vector")
            if ffn_stop == "hsilu":
                P.barrier()
                continue
            P.do("vector", lambda v: v.tensor_copy(F[:, F_CT:F_CT + 1024], A[:, A_M:A_M + 1024]))
            P.selfwait("vector")
            P.wait("tensor", vS, vS.v)
            P.wait("tensor", sS, sS.v)
            tl = None
            for k in range(8):
                tl = P.do("tensor", lambda t_, k=k: t_.transpose(
                    PS_C[:, 128 * k:128 * k + 128], F[:, F_CT + 128 * k:F_CT + 128 * k + 128],
                    idf[:, :]), inc=(k == 7))
            P.wait("vector", tS, tl)
            for k in range(8):
                P.do("vector", lambda v, k=k: v.tensor_reduce(
                    F[:, F_MX2 + k:F_MX2 + k + 1], PS_C[:, 128 * k:128 * k + 128], AX.X, OP.max),
                    inc=(k == 7))
            P.selfwait("vector")
            dov(lambda v: v.tensor_scalar_max(F[:, F_MX2:F_MX2 + 8], F[:, F_MX2:F_MX2 + 8], 1e-5))
            dov(lambda v: v.reciprocal(F[:, F_S2:F_S2 + 8], F[:, F_MX2:F_MX2 + 8]))
            dov(lambda v: v.tensor_scalar(F[:, F_S2:F_S2 + 8], F[:, F_S2:F_S2 + 8],
                                          127.0, None, OP.mult))
            dov(lambda v: v.tensor_copy(A[0:128, A_S2BC:A_S2BC + 8], F[:, F_S2:F_S2 + 8]))
            dov(lambda v: v.tensor_copy(F[:, F_S2P:F_S2P + 8], A[0:128, A_S2BC:A_S2BC + 8]))
            dov(lambda v: v.reciprocal(F[:, F_ESC:F_ESC + 8], F[:, F_S2P:F_S2P + 8]))
            dov(lambda v: v.tensor_scalar(F[:, F_ESC:F_ESC + 8], F[:, F_ESC:F_ESC + 8],
                                          mwb[:, 1:2], None, OP.mult))
            P.wait("tensor", vS, vS.v)
            tl = None
            for k in range(8):
                tl = P.do("tensor", lambda t_, k=k: t_.transpose(
                    PS_A[0:1, 128 * k:128 * k + 128], F[:, F_S2P + k:F_S2P + k + 1],
                    idf[:, :]), inc=(k == 7))
            P.wait("scalar", tS, tl)
            P.do("scalar", lambda s: s.activation(A[0:1, A_S2R:A_S2R + 1024], PS_A[0:1, 0:1024],
                                                  AF.Identity))
            P.wait("tensor", sS, sS.v)
            tl = None
            for half in range(2):
                tl = P.do("tensor", lambda t_, half=half: t_.matmul(
                    PS_B[:, 512 * half:512 * half + 512], onesb[0:1, :],
                    A[0:1, A_S2R + 512 * half:A_S2R + 512 * half + 512],
                    start=True, stop=True), inc=(half == 1))
            P.wait("scalar", tS, tl)
            P.do("scalar", lambda s: s.activation(A[:, A_S2BC:A_S2BC + 1024], PS_B[:, 0:1024],
                                                  AF.Identity))
            P.wait("vector", sS, sS.v)
            P.wait("tensor", vS, vS.v)
            P.wait("tensor", sS, sS.v)
            n2done = {}
            oct_ps = [PS_A[:, 0:512], PS_A[:, 512:1024], PS_A[:, 1024:1536],
                      PS_B[:, 0:512], PS_B[:, 512:1024], PS_B[:, 1024:1536],
                      PS_C[:, 0:512], PS_C[:, 512:1024]]
            slot_guard = {0: w2_guard[0], 1: w2_guard[0]}
            for h2 in range(2):
                fills = {}
                for g in (0, 1):
                    P.wait("sync", tS, slot_guard[g % 2])
                    fills[g] = P.dma(
                        "sync", A[:, A_W2S + 2048 * (g % 2):A_W2S + 2048 * (g % 2) + 2048],
                        scr_w2[:, 16384 * h2 + 2048 * g:16384 * h2 + 2048 * g + 2048],
                        q_w2[g % 2])
                grp_last = {}
                for cj in range(NCJ):
                    g, gi = cj // 4, cj % 4
                    par = g % 2
                    if gi == 0 and g >= 1 and g + 1 < 8:
                        P.wait("sync", tS, grp_last[g - 1])
                        fills[g + 1] = P.dma(
                            "sync",
                            A[:, A_W2S + 2048 * ((g + 1) % 2):A_W2S + 2048 * ((g + 1) % 2) + 2048],
                            scr_w2[:, 16384 * h2 + 2048 * (g + 1):16384 * h2 + 2048 * (g + 1) + 2048],
                            q_w2[(g + 1) % 2])
                    if h2 == 0:
                        for h in range(2):
                            P.do("vector", lambda v, cj=cj, h=h: v.tensor_tensor(
                                sint[:, :],
                                A[:, A_HS + 1024 * cj + 512 * h:A_HS + 1024 * cj + 512 * h + 512],
                                A[:, A_S2BC + 512 * h:A_S2BC + 512 * h + 512], OP.mult),
                                inc=False)
                            n2done[cj] = P.do("vector", lambda v, cj=cj, h=h: v.tensor_copy(
                                A[:, A_HS + 1024 * cj + 512 * h:A_HS + 1024 * cj + 512 * h + 512],
                                sint[:, :]))
                        P.wait("tensor", vS, n2done[cj])
                    w2sl = A[:, A_W2S + 2048 * par + 512 * gi:A_W2S + 2048 * par + 512 * gi + 512]
                    P.wait("tensor", q_w2[par], fills[g])
                    last = None
                    for tt in range(8):
                        pp = oct_ps[tt]
                        last = P.do("tensor", lambda t_, cj=cj, tt=tt, pp=pp, w2sl=w2sl: t_.matmul(
                            pp, A[:, A_HS + 1024 * cj + 128 * tt:A_HS + 1024 * cj + 128 * tt + 128],
                            w2sl, start=(cj == 0), stop=(cj == NCJ - 1)), inc=(tt == 7))
                    if gi == 3:
                        grp_last[g] = last
                slot_guard[0] = slot_guard[1] = last
                w2_guard[0] = last
                P.wait("vector", tS, last)
                for tt in range(8):
                    c_glob = 8 * sblk + tt
                    pp = oct_ps[tt]
                    xsl = x_sb[:, 1024 * c_glob + 512 * h2:1024 * c_glob + 512 * h2 + 512]
                    P.do("vector", lambda v, pp=pp, xsl=xsl, tt=tt: v.scalar_tensor_tensor(
                        xsl, pp, F[:, F_ESC + tt:F_ESC + tt + 1], xsl, OP.mult, OP.add),
                        inc=(tt == 7))
                P.wait("tensor", vS, vS.v)
            marks.append(vS.v)
            if sblk == 0 and mid_hook is not None:
                mid_hook()
        return marks

    def attn_stage(seq, pre_done=0):
        P.wait("sync", tS, tS.v)
        P.dma("sync", A[:, 37120:37120 + 8192], scr_wv[:, :], q_wv)
        ln_stats(for_ffn=False, pre_done=pre_done)
        evacd = {}
        on_done = {}
        sdone_holder = {}

        def emit_v(tt):
            pv = PS_B
            P.wait("tensor", q_wv, q_wv.v)
            P.wait("tensor", sS, max(evacd.get(tt - 1, 0), sdone_holder[tt]))
            last = None
            for k in range(8):
                for half in range(2):
                    P.do("tensor", lambda t_, k=k, half=half, tt=tt, pv=pv: t_.matmul(
                        pv[:, 512 * half:512 * half + 512],
                        A[:, A_N1T + 2048 * k + 128 * tt:A_N1T + 2048 * k + 128 * tt + 128],
                        A[:, 37120 + 1024 * k + 512 * half:37120 + 1024 * k + 512 * half + 512],
                        start=(k == 0), stop=False), inc=False)
            for half in range(2):
                last = P.do("tensor", lambda t_, half=half, pv=pv: t_.matmul(
                    pv[:, 512 * half:512 * half + 512], onesb[0:1, :],
                    bvb[0:1, 512 * half:512 * half + 512], start=False, stop=True),
                    inc=(half == 1))
            P.wait("scalar", tS, last)
            P.wait("scalar", vS, on_done.get(tt - 1, 0))
            vg = r2(A, AT_VG + 1040 * tt, 1040, 65)
            evacd[tt] = P.do("scalar", lambda s, vg=vg, pv=pv: s.activation(
                vg[:, :, 0:64], r2(pv, 0, 1024, 64), AF.Identity))
            P.wait("vector", sS, evacd[tt])
            on_done[tt] = P.do("vector", lambda v, vg=vg: v.memset(vg[:, :, 64:65], 1.0))

        def chase_v(c):
            sdone_holder.update(xh_partial())
            if c >= 1:
                emit_v(c - 1)
            if c == NCH - 1:
                emit_v(c)

        xh_ref = [None]

        def xh_partial():
            return xh_ref[0] if xh_ref[0] is not None else {}

        xh_ref[0] = sdone_cap = {}
        def hook(c):
            sdone_cap.update({c: None})
            chase_v(c)

        xh_done = make_xhatT(for_ffn=False, interleave=None, v_hook=emit_v,
                             v_sdone=sdone_holder)
        P.wait("tensor", sS, sS.v)
        P.wait("tensor", vS, vS.v)
        attn_pend = [None]
        prev_ce = [0]
        prev_expd = [{}]
        qkfill = {}
        P.wait("sync", tS, tS.v)
        for hp in (0, 1):
            par = hp % 2
            P.dma("sync", A[:, AT_WQS + 2048 * par:AT_WQS + 2048 * par + 1024],
                  scr_wq[:, 1024 * hp:1024 * hp + 1024], q_qk[par])
            qkfill[hp] = P.dma("sync", A[:, AT_WQS + 2048 * par + 1024:AT_WQS + 2048 * par + 2048],
                               scr_wk[:, 1024 * hp:1024 * hp + 1024], q_qk[par])
        for hp in range(8):
            par = hp % 2
            P.wait("tensor", q_qk[par], qkfill[hp])
            P.wait("tensor", vS, vS.v)
            pevac = {}
            for which, dstc, scale_t, bias_t in ((0, AT_Q, scq, bqs), (1, AT_K, sck, bks)):
                wsl = A[:, AT_WQS + 2048 * par + 1024 * which:
                        AT_WQS + 2048 * par + 1024 * which + 1024]
                for b in range(4):
                    g_lin = 4 * which + b
                    ph = PS_C[:, 512 * (g_lin % 2):512 * (g_lin % 2) + 512]
                    P.wait("tensor", sS, pevac.get(g_lin - 2, 0))
                    last = None
                    for k in range(8):
                        last = P.do("tensor", lambda t_, k=k, b=b, ph=ph, wsl=wsl: t_.matmul(
                            ph, wsl[:, 128 * k:128 * k + 128],
                            A[:, A_N1T + 2048 * k + 512 * b:A_N1T + 2048 * k + 512 * b + 512],
                            start=(k == 0), stop=(k == 7)), inc=(k == 7))
                    P.wait("scalar", tS, last)
                    pevac[g_lin] = P.do("scalar", lambda s, ph=ph, b=b, dstc=dstc,
                                             scale_t=scale_t, bias_t=bias_t, hp=hp: s.activation(
                        A[:, dstc + 512 * b:dstc + 512 * b + 512], ph, AF.Identity,
                        bias=bias_t[:, hp:hp + 1], scale=scale_t[:, hp:hp + 1]))
            if hp + 2 < 8:
                P.wait("sync", tS, tS.v)
                P.dma("sync", A[:, AT_WQS + 2048 * par:AT_WQS + 2048 * par + 1024],
                      scr_wq[:, 1024 * (hp + 2):1024 * (hp + 2) + 1024], q_qk[par])
                qkfill[hp + 2] = P.dma(
                    "sync", A[:, AT_WQS + 2048 * par + 1024:AT_WQS + 2048 * par + 2048],
                    scr_wk[:, 1024 * (hp + 2):1024 * (hp + 2) + 1024], q_qk[par])
            P.wait("tensor", sS, sS.v)
            pend = attn_pend[0]

            def ctx_ps(hi):
                return (PS_A if hi == 0 else PS_B)[0:65, 1024:1536]

            def emit_finalize():
                ph, pb, ce1 = attn_pend[0]
                attn_pend[0] = None
                P.wait("tensor", sS, sS.v)
                P.wait("tensor", vS, max(vS.v if False else ce1, ce1))
                P.wait("tensor", vS, vS.v)
                tl = None
                for hi in range(2):
                    base = F_CTXA if hi == 0 else F_CTXS
                    pdst = 0 if hi == 0 else 512
                    for g in range(4):
                        tl = P.do("tensor", lambda t_, g=g, base=base, pdst=pdst: t_.transpose(
                            PS_C[:, pdst + 65 * g:pdst + 65 * g + 65],
                            F[0:65, base + 128 * g:base + 128 * g + 128],
                            idf[0:65, 0:65]), inc=(hi == 1 and g == 3))
                P.wait("vector", tS, tl)
                for hi in range(2):
                    pdst = 0 if hi == 0 else 512
                    sums = bass.AP(PS_C, pdst + 64, [[1024, 128], [65, 4]])
                    P.do("vector", lambda v, sums=sums, hi=hi: v.reciprocal(
                        F[:, F_RV + 4 * hi:F_RV + 4 * hi + 4], sums), inc=(hi == 1))
                P.selfwait("vector")
                for hi in range(2):
                    pdst = 0 if hi == 0 else 512
                    for g in range(4):
                        c_glob = 4 * pb + g
                        xsl = x_sb[:, 1024 * c_glob + 64 * (2 * ph + hi):
                                   1024 * c_glob + 64 * (2 * ph + hi) + 64]
                        P.do("vector", lambda v, g=g, xsl=xsl, pdst=pdst, hi=hi:
                             v.scalar_tensor_tensor(
                                 xsl, PS_C[:, pdst + 65 * g:pdst + 65 * g + 64],
                                 F[:, F_RV + 4 * hi + g:F_RV + 4 * hi + g + 1],
                                 xsl, OP.mult, OP.add), inc=(hi == 1 and g == 3))
                return vS.v

            for blk in range(4):
                expd = dict(prev_expd[0])
                ctx_last = {}
                for tk in range(NCH):
                    if tk == 2 and attn_pend[0] is not None:
                        emit_finalize()
                    pp = PS_A if tk % 2 == 0 else PS_B
                    P.wait("tensor", sS, expd.get(tk - 2, 0))
                    P.do("tensor", lambda t_, tk=tk, blk=blk, pp=pp: t_.matmul(
                        pp[:, 0:512], A[0:64, AT_K + 128 * tk:AT_K + 128 * tk + 128],
                        A[0:64, AT_Q + 512 * blk:AT_Q + 512 * blk + 512],
                        start=True, stop=True, tile_position=(0, 0)), inc=False)
                    m1 = P.do("tensor", lambda t_, tk=tk, blk=blk, pp=pp: t_.matmul(
                        pp[:, 512:1024], A[64:128, AT_K + 128 * tk:AT_K + 128 * tk + 128],
                        A[64:128, AT_Q + 512 * blk:AT_Q + 512 * blk + 512],
                        start=True, stop=True, tile_position=(64, 0)))
                    if tk >= 1:
                        if tk == 1:
                            P.wait("tensor", vS, prev_ce[0])
                        ptp = A[:, PT_SLOT[(tk - 1) % 4]:PT_SLOT[(tk - 1) % 4] + 1024]
                        P.wait("tensor", sS, expd[tk - 1])
                        for hi in range(2):
                            vg = A[:, AT_VG + 1040 * (tk - 1) + 65 * (2 * hp + hi):
                                   AT_VG + 1040 * (tk - 1) + 65 * (2 * hp + hi) + 65]
                            ctx_last[tk - 1] = P.do("tensor", lambda t_, vg=vg, hi=hi, ptp=ptp,
                                                    pt_=tk - 1: t_.matmul(
                                ctx_ps(hi), vg, ptp[:, 512 * hi:512 * hi + 512],
                                start=(pt_ == 0), stop=(pt_ == NCH - 1)), inc=(hi == 1))
                    pt = A[:, PT_SLOT[tk % 4]:PT_SLOT[tk % 4] + 1024]
                    P.wait("scalar", tS, m1)
                    if tk >= 4:
                        P.wait("scalar", tS, ctx_last[tk - 4])
                    expd[tk] = P.do("scalar", lambda s, pp=pp, pt=pt: s.activation(
                        pt, pp[:, 0:1024], AF.Exp, scale=0.125))
                ptp = A[:, PT_SLOT[(NCH - 1) % 4]:PT_SLOT[(NCH - 1) % 4] + 1024]
                P.wait("tensor", sS, expd[NCH - 1])
                for hi in range(2):
                    vg = A[:, AT_VG + 1040 * (NCH - 1) + 65 * (2 * hp + hi):
                           AT_VG + 1040 * (NCH - 1) + 65 * (2 * hp + hi) + 65]
                    ctx_last[NCH - 1] = P.do("tensor", lambda t_, vg=vg, hi=hi, ptp=ptp: t_.matmul(
                        ctx_ps(hi), vg, ptp[:, 512 * hi:512 * hi + 512],
                        start=False, stop=True), inc=(hi == 1))
                ce1 = None
                for hi in range(2):
                    P.wait("vector", tS, ctx_last[NCH - 1])
                    base = F_CTXA if hi == 0 else F_CTXS
                    ce1 = P.do("vector", lambda v, hi=hi, base=base: v.tensor_copy(
                        F[0:65, base:base + 512], ctx_ps(hi)))
                attn_pend[0] = (hp, blk, ce1)
                prev_ce[0] = ce1
                prev_expd[0] = {14 - 16: expd[14], 15 - 16: expd[15]}
        if attn_pend[0] is not None:
            emit_finalize()
        w1_guard[0] = max(w1_guard[0], tS.v)

    steps = [(0, "A"), (0, "T"), (0, "C"), (1, "A"), (1, "T"), (1, "C")][:n_steps]
    loaded = set()
    if n_steps == 0:
        src0 = bass.AP(xin, 0, [[D, 128], [128 * D, NCH], [1, D]])
        P.dma("gpsimd", bass.AP(x_sb, 0, [[NCH * D, 128], [D, NCH], [1, D]]), src0, q_x[0])
        for e in ENGS:
            P.wait(e, q_x[0], q_x[0].v)
        loaded.add(0)
    for si, (seq, st) in enumerate(steps):
        if seq not in loaded:
            if seq == 1:
                for e in ENGS:
                    P.wait(e, q_out[0], q_out[0].v)
            src = bass.AP(xin, seq * S * D, [[D, 128], [128 * D, NCH], [1, D]])
            P.dma("gpsimd", bass.AP(x_sb, 0, [[NCH * D, 128], [D, NCH], [1, D]]), src, q_x[seq])
            for e in ENGS:
                P.wait(e, q_x[seq], q_x[seq].v)
            loaded.add(seq)
        marks = None
        if st in ("A", "C"):
            nxt = steps[si + 1][1] if si + 1 < len(steps) else None
            hook = None
            if nxt == "T":
                def hook():
                    ln_stats_chunks(False, 0, 8, F_MX)
            marks = ffn_stage(seq, mid_hook=hook)
        else:
            attn_stage(seq, pre_done=0 if steps[si - 1][1] not in ("A",) else 8)
        last_of_seq = (si == len(steps) - 1) or (steps[si + 1][0] != seq)
        if last_of_seq:
            if marks is not None and len(marks) == 2:
                for half, mk in enumerate(marks):
                    P.wait("gpsimd", vS, mk)
                    dst = bass.AP(xout, seq * S * D + half * 8 * 128 * D,
                                  [[D, 128], [128 * D, 8], [1, D]])
                    srcp = bass.AP(x_sb, half * 8 * D, [[NCH * D, 128], [D, 8], [1, D]])
                    P.dma("gpsimd", dst, srcp, q_out[seq])
            else:
                P.barrier()
                dst = bass.AP(xout, seq * S * D, [[D, 128], [128 * D, NCH], [1, D]])
                P.dma("gpsimd", dst, bass.AP(x_sb, 0, [[NCH * D, 128], [D, NCH], [1, D]]),
                      q_out[seq])
    if dump is not None or n_steps == 0:
        P.barrier()
        if dump is None:
            dst = bass.AP(xout, 0, [[D, 128], [128 * D, NCH], [1, D]])
            P.dma("gpsimd", dst, bass.AP(x_sb, 0, [[NCH * D, 128], [D, NCH], [1, D]]), q_out[0])
        else:
            which, lo, ncols = dump
            tsrc = {"A": A, "F": F, "X": x_sb}[which]
            dst = bass.AP(xout, 0, [[ncols, 128], [1, ncols]])
            P.dma("gpsimd", dst, tsrc[:, lo:lo + ncols], q_out[0])
    for seq in (0, 1):
        if q_out[seq].v:
            P.wait("gpsimd", q_out[seq], q_out[seq].v)

    with nc.Block() as block:
        @block.sync
        def _(eng):
            for fn in P.ops["sync"]:
                fn(eng)

        @block.tensor
        def _(eng):
            for fn in P.ops["tensor"]:
                fn(eng)

        @block.vector
        def _(eng):
            for fn in P.ops["vector"]:
                fn(eng)

        @block.scalar
        def _(eng):
            for fn in P.ops["scalar"]:
                fn(eng)

        @block.gpsimd
        def _(eng):
            for fn in P.ops["gpsimd"]:
                fn(eng)

    es.close()
    return nc


def kernel(L0_attn_hidden_states, L0_attn_Wq, L0_attn_bq, L0_attn_Wk, L0_attn_bk,
           L0_attn_Wv, L0_attn_bv, L0_ffn_w1, L0_ffn_conv_w, L0_ffn_conv_b, L0_ffn_w2,
           _n_steps=6):
    x = np.ascontiguousarray(L0_attn_hidden_states, dtype=np.float32)
    nc = build_program(_n_steps)
    cf = lambda a: np.ascontiguousarray(a, dtype=np.float32)
    in_maps = []
    for c in range(8):
        in_maps.append({
            "xin": cf(x[2 * c:2 * c + 2]),
            "wq": cf(L0_attn_Wq), "wk": cf(L0_attn_Wk), "wv": cf(L0_attn_Wv),
            "bq": cf(L0_attn_bq), "bk": cf(L0_attn_bk), "bv": cf(L0_attn_bv),
            "w1": cf(L0_ffn_w1), "w2": cf(L0_ffn_w2),
            "cw": cf(L0_ffn_conv_w), "cb": cf(L0_ffn_conv_b),
        })
    res = run_bass_kernel_spmd(nc, in_maps, core_ids=list(range(8)))
    out = np.concatenate([r["xout"] for r in res.results], axis=0)
    return out


# version 21
# speedup vs baseline: 1.0058x; 1.0058x over previous
import os
import numpy as np
from contextlib import ExitStack
import concourse.bass as bass
import concourse.mybir as mybir
from concourse.bass_utils import run_bass_kernel_spmd

F32 = mybir.dt.float32
BF16 = mybir.dt.bfloat16
AF = mybir.ActivationFunctionType
OP = mybir.AluOpType
AX = mybir.AxisListType

MAGIC = 3 * 2.0 ** 22
MAGIC64 = 3 * 2.0 ** 51
D = 1024
S = 2048
NCH = 16
DH = 4096
NCJ = 32
EPS = 1e-5

A_N1T = 0
A_HS = 16384
A_W1S = 49152
A_W2S = 53248
A_S2BC = 57344
A_M = 58368
A_S2R = 58368
A_TOT = 59392
AT_VG = 16384
AT_Q = 33024
AT_K = 35072
AT_PT = 37120
AT_WQS = 39168
AT_WVS = 43264
PT_SLOT = (37120, 38144, 49152, 50176)
P_STG = 0
P_TERN = 32768

F_HRAW = (0, 1026)
F_CT = 2052
F_BN = 3076
F_MV = 3088
F_MX = 3120
F_MN = 3136
F_R = 3152
F_MRB = 3168
F_SA = 3184
F_BA = 3200
F_IS1 = 3216
F_TA = 3232
F_TB = 3248
F_MX2 = 3264
F_S2 = 3272
F_S2P = 3280
F_ESC = 3288
F_RV = 3296
F_SQ = 3300
F_VP = 3316
F_W = 3332
F_SCQ = 3364
F_ONE = 3372
F_WT = 3392
F_BNB = 3392
F_CTXS = 4416
F_CTXA = 3584
F_XQT = 3392
F_TOT = 5440

ENGS = ("sync", "tensor", "vector", "scalar", "gpsimd")
SIM_SERIAL = os.environ.get("BASS_SIM_FP64") == "1"


class Sem:
    __slots__ = ("h", "v", "name")

    def __init__(self, nc, name):
        self.h = nc.alloc_semaphore(name=name)
        self.v = 0
        self.name = name


class Prog:
    def __init__(self, nc):
        self.nc = nc
        self.ops = {e: [] for e in ENGS}
        self.es = {e: Sem(nc, "es_" + e) for e in ENGS}
        self._lastwait = {}

    def do(self, e, fn, inc=True):
        if SIM_SERIAL:
            s = self.es[e]
            s.v += 1
            self.ops[e].append(lambda eng, f=fn, h=s.h: f(eng).then_inc(h, 1))
            self.ops[e].append(lambda eng, h=s.h, v=s.v: eng.wait_ge(h, v))
            self._lastwait[(e, s.name)] = s.v
            return s.v
        if inc:
            s = self.es[e]
            s.v += 1
            self.ops[e].append(lambda eng, f=fn, h=s.h: f(eng).then_inc(h, 1))
            return s.v
        self.ops[e].append(fn)
        return None

    def dma(self, e, out, in_, dsem, slow=False):
        dsem.v += 16
        self.ops[e].append(
            lambda eng, o=out, i=in_, h=dsem.h, sl=slow: eng.dma_start(
                out=o, in_=i, allow_slow_non_contiguous=sl).then_inc(h, 16))
        return dsem.v

    def wait(self, e, sem, val):
        if not val:
            return
        key = (e, sem.name)
        if self._lastwait.get(key, -1) >= val:
            return
        self._lastwait[key] = val
        self.ops[e].append(lambda eng, h=sem.h, v=val: eng.wait_ge(h, v))

    def selfwait(self, e):
        self.wait(e, self.es[e], self.es[e].v)

    def barrier(self, dma_sems=()):
        snap = {e: self.es[e].v for e in ENGS}
        for e in ENGS:
            for o in ENGS:
                if o != e:
                    self.wait(e, self.es[o], snap[o])
            for ds in dma_sems:
                self.wait(e, ds, ds.v)


def r2(t, lo, n, inner):
    return t[:, lo:lo + n].rearrange("p (a b) -> p a b", b=inner)


def build_program(n_steps=6, dump=None, ffn_stop=None):
    nc = bass.Bass("TRN2", target_bir_lowering=False)
    es = ExitStack()
    nc._keepalive_es = es

    xin = nc.dram_tensor("xin", [2, S, D], F32, kind="ExternalInput")
    wq_d = nc.dram_tensor("wq", [D, D], F32, kind="ExternalInput")
    wk_d = nc.dram_tensor("wk", [D, D], F32, kind="ExternalInput")
    wv_d = nc.dram_tensor("wv", [D, D], F32, kind="ExternalInput")
    bq_d = nc.dram_tensor("bq", [D], F32, kind="ExternalInput")
    bk_d = nc.dram_tensor("bk", [D], F32, kind="ExternalInput")
    bv_d = nc.dram_tensor("bv", [D], F32, kind="ExternalInput")
    w1_d = nc.dram_tensor("w1", [DH, D], F32, kind="ExternalInput")
    w2_d = nc.dram_tensor("w2", [D, DH], F32, kind="ExternalInput")
    cw_d = nc.dram_tensor("cw", [DH, 1, 3], F32, kind="ExternalInput")
    cb_d = nc.dram_tensor("cb", [DH], F32, kind="ExternalInput")
    xout = nc.dram_tensor("xout", [2, S, D], F32, kind="ExternalOutput")
    scr_w1 = nc.dram_tensor("scr_w1", [128, 32768], BF16)
    scr_w2 = nc.dram_tensor("scr_w2", [128, 32768], BF16)
    scr_wq = nc.dram_tensor("scr_wq", [128, 8192], BF16)
    scr_wk = nc.dram_tensor("scr_wk", [128, 8192], BF16)
    scr_wv = nc.dram_tensor("scr_wv", [128, 8192], BF16)

    def sb(name, shape, dt):
        return es.enter_context(nc.sbuf_tensor(name, shape, dt))

    def ps(name, shape, dt):
        return es.enter_context(nc.psum_tensor(name, shape, dt))

    x_sb = sb("x_sb", [128, NCH * D], F32)
    A = sb("A", [128, A_TOT], BF16)
    F = sb("F", [128, F_TOT], F32)
    idf = sb("idf", [128, 128], F32)
    onesb = sb("onesb", [1, 128], BF16)
    onesf = sb("onesf", [1, 128], F32)
    cwp = sb("cwp", [128, 96], F32)
    cbp = sb("cbp", [128, 32], F32)
    bqs = sb("bqs", [128, 8], F32)
    bks = sb("bks", [128, 8], F32)
    scq = sb("scq", [128, 8], F32)
    sck = sb("sck", [128, 8], F32)
    bvb = sb("bvb", [1, D], BF16)
    sint = sb("sint", [128, 512], mybir.dt.int32)
    mwb = sb("mwb", [128, 4], F32)

    PS_A = ps("PS_A", [128, 1536], F32)
    PS_B = ps("PS_B", [128, 1536], F32)
    PS_C = ps("PS_C", [128, 1024], F32)

    P = Prog(nc)

    def emit_round(src_ap):
        if SIM_SERIAL:
            P.do("vector", lambda v, s=src_ap: v.tensor_scalar(
                s, s, MAGIC64, MAGIC64, OP.add, OP.subtract))
            return src_ap
        P.do("vector", lambda v, s=src_ap: v.tensor_copy(sint[:, :], s), inc=False)
        return sint[:, :]

    q_x = [Sem(nc, "q_x0"), Sem(nc, "q_x1")]
    q_out = [Sem(nc, "q_o0"), Sem(nc, "q_o1")]
    q_pw = [Sem(nc, "q_pw0"), Sem(nc, "q_pw1")]
    q_scr = Sem(nc, "q_scr")
    q_sm = Sem(nc, "q_sm")
    q_bv = Sem(nc, "q_bv")
    q_w1 = [Sem(nc, "q_w1a"), Sem(nc, "q_w1b")]
    q_w2 = [Sem(nc, "q_w2a"), Sem(nc, "q_w2b")]
    q_qk = [Sem(nc, "q_qka"), Sem(nc, "q_qkb")]
    q_wv = Sem(nc, "q_wv")

    tS, vS, sS, gS = P.es["tensor"], P.es["vector"], P.es["scalar"], P.es["gpsimd"]


    P.do("gpsimd", lambda g: g.iota(F[:, 0:128], [[1, 128]], channel_multiplier=0,
                                    allow_small_or_imprecise_dtypes=True))
    P.do("gpsimd", lambda g: g.iota(F[:, 128:256], [[0, 128]], channel_multiplier=1,
                                    allow_small_or_imprecise_dtypes=True))
    P.wait("vector", gS, gS.v)
    P.do("vector", lambda v: v.tensor_tensor(idf[:, :], F[:, 0:128], F[:, 128:256], OP.is_equal))
    P.do("vector", lambda v: v.memset(onesb[:, :], 1.0))
    P.do("vector", lambda v: v.memset(onesf[:, :], 1.0))
    P.do("vector", lambda v: v.memset(F[:, F_ONE:F_ONE + 1], 1.0))

    P.dma("sync", bqs[:, :], bass.AP(bq_d, 0, [[1, 128], [128, 8]]), q_sm, slow=True)
    P.dma("sync", bks[:, :], bass.AP(bk_d, 0, [[1, 128], [128, 8]]), q_sm, slow=True)
    P.dma("gpsimd", bvb[:, :], bass.AP(bv_d, 0, [[1024, 1], [1, 1024]]), q_bv)
    P.dma("sync", bass.AP(cwp, 0, [[96, 128], [32, 3], [1, 32]]),
          bass.AP(cw_d, 0, [[3, 128], [1, 3], [384, 32]]), q_sm, slow=True)
    P.dma("sync", cbp[:, :], bass.AP(cb_d, 0, [[1, 128], [128, 32]]), q_sm, slow=True)

    def w1_src(j):
        return bass.AP(w1_d, j * 128 * 1024, [[1024, 128], [1, 1024]])

    def w2_src(piece):
        i, qq = piece // 4, piece % 4
        return bass.AP(w2_d, i * 128 * 4096 + 1024 * qq, [[4096, 128], [1, 1024]])

    def wq_src(j):
        return bass.AP(wq_d, j * 128 * 1024, [[1024, 128], [1, 1024]])

    def wk_src(j):
        return bass.AP(wk_d, j * 128 * 1024, [[1024, 128], [1, 1024]])

    def wv_src(j):
        return bass.AP(wv_d, j * 128 * 1024, [[1024, 128], [1, 1024]])

    PREP_ST = F_MX
    p1 = [(w1_src, 32, OP.add, F_W),
          (w2_src, 32, OP.add, PREP_ST),
          (wq_src, 8, OP.max, PREP_ST + 32),
          (wk_src, 8, OP.max, PREP_ST + 40),
          (wv_src, 8, OP.max, PREP_ST + 48)]
    d1 = {}
    gi_ = 0
    p1_slots = [F_WT, F_WT + 1024, F_HRAW[0], F_HRAW[1]]
    q_p1 = [q_pw[0], q_pw[1], q_w1[0], q_w1[1]]
    P.wait("sync", vS, vS.v)
    for srcf, np_, op_, base in p1:
        for j in range(np_):
            par = gi_ % 4
            wt = F[:, p1_slots[par]:p1_slots[par] + 1024]
            P.wait("sync", vS, d1.get(gi_ - 4, 0))
            P.dma("sync", wt, srcf(j), q_p1[par])
            P.wait("vector", q_p1[par], q_p1[par].v)
            d1[gi_] = P.do("vector", lambda v, wt=wt, base=base, j=j, op_=op_: v.tensor_reduce(
                F[:, base + j:base + j + 1], wt, AX.X, op_, apply_absolute_value=True))
            gi_ += 1
    pass1_v_end = vS.v

    def mean_finalize(sum_base, npieces, mw_col, sw_col):
        P.selfwait("vector")
        P.do("vector", lambda v: v.tensor_reduce(
            F[:, F_TA:F_TA + 1], F[:, sum_base:sum_base + npieces], AX.X, OP.add))
        P.selfwait("vector")
        P.wait("tensor", vS, vS.v)
        P.do("tensor", lambda t: t.matmul(PS_C[0:1, 0:1], F[:, F_TA:F_TA + 1],
                                          F[:, F_ONE:F_ONE + 1], start=True, stop=True))
        P.wait("scalar", tS, tS.v)
        P.do("scalar", lambda s: s.activation(F[0:1, F_TA:F_TA + 1], PS_C[0:1, 0:1],
                                              AF.Identity, scale=1.0 / (4096.0 * 1024.0)))
        P.wait("vector", sS, sS.v)
        P.do("vector", lambda v: v.tensor_scalar_max(
            F[0:1, F_TA:F_TA + 1], F[0:1, F_TA:F_TA + 1], 1e-5))
        P.selfwait("vector")
        P.do("vector", lambda v: v.reciprocal(F[0:1, F_TB:F_TB + 1], F[0:1, F_TA:F_TA + 1]))
        P.selfwait("vector")
        P.wait("tensor", vS, vS.v)
        P.do("tensor", lambda t: t.matmul(PS_C[:, 4:5], onesf[0:1, :], F[0:1, F_TA:F_TA + 1],
                                          start=True, stop=True))
        P.do("tensor", lambda t: t.matmul(PS_C[:, 5:6], onesf[0:1, :], F[0:1, F_TB:F_TB + 1],
                                          start=True, stop=True))
        P.wait("scalar", tS, tS.v)
        P.do("scalar", lambda s: s.activation(mwb[:, mw_col:mw_col + 1], PS_C[:, 4:5], AF.Identity,
                                              scale=(0.5 if mw_col == 1 else 1.0)))
        P.do("scalar", lambda s: s.activation(mwb[:, sw_col:sw_col + 1], PS_C[:, 5:6], AF.Identity))
        P.wait("vector", sS, sS.v)

    mean_finalize(F_W, 32, 0, 2)
    mean_finalize(PREP_ST, 32, 1, 3)
    ATT_INV = PREP_ST + 56
    for idx, (mx_base, sc_dst) in enumerate(((PREP_ST + 32, scq), (PREP_ST + 40, sck),
                                             (PREP_ST + 48, None))):
        P.do("vector", lambda v, mx_base=mx_base, idx=idx: v.tensor_scalar(
            F[:, F_TA:F_TA + 8], F[:, mx_base:mx_base + 8], 1.0 / 127.0, 1e-8,
            OP.mult, OP.max))
        P.selfwait("vector")
        if sc_dst is not None:
            P.do("vector", lambda v, sc_dst=sc_dst: v.tensor_copy(sc_dst[:, :], F[:, F_TA:F_TA + 8]))
        else:
            P.do("vector", lambda v: v.tensor_copy(F[:, ATT_INV + 24:ATT_INV + 32],
                                                   F[:, F_TA:F_TA + 8]))
        P.do("vector", lambda v, idx=idx: v.reciprocal(
            F[:, ATT_INV + 8 * idx:ATT_INV + 8 * idx + 8], F[:, F_TA:F_TA + 8]))
        P.selfwait("vector")

    p2_first = [True]
    d2v, d2s, d2t = {}, {}, {}
    g2 = [0]
    tern_par = [0]

    def p2_generic(srcf, npieces, quant_fn, evac_dst, scr, scr_cols, stage_cols):
        if p2_first[0]:
            P.wait("sync", vS, pass1_v_end)
            p2_first[0] = False
        P.wait("scalar", q_scr, q_scr.v)
        for j in range(npieces):
            gj = g2[0]
            g2[0] += 1
            par = gj % 2
            tern_par[0] = par
            wt = F[:, F_WT + 1024 * par:F_WT + 1024 * par + 1024]
            P.wait("sync", vS, d2v.get(gj - 2, 0))
            P.dma("sync", wt, srcf(j), q_pw[par])
            P.wait("vector", q_pw[par], q_pw[par].v)
            tern = F[:, F_HRAW[par]:F_HRAW[par] + 1024]
            P.wait("vector", tS, d2t.get(gj - 1 if SIM_SERIAL else gj - 2, 0))
            d2v[gj] = quant_fn(j, wt, tern)
            PB = PS_C if par == 0 else PS_A
            P.wait("tensor", vS, d2v[gj])
            P.wait("tensor", sS, d2s.get(gj - 2, 0))
            tl = None
            for k in range(8):
                tl = P.do("tensor", lambda t_, k=k, PB=PB, tern=tern: t_.transpose(
                    PB[:, 128 * k:128 * k + 128], tern[:, 128 * k:128 * k + 128], idf[:, :]),
                    inc=(k == 7))
            d2t[gj] = tl
            P.wait("scalar", tS, tl)
            d2s[gj] = P.do("scalar", lambda s, j=j, PB=PB: s.activation(
                evac_dst(j), PB[:, 0:1024], AF.Identity))
        P.wait("gpsimd", sS, sS.v)
        P.dma("gpsimd", scr[:, 0:scr_cols], A[:, P_STG:P_STG + scr_cols], q_scr)

    def tern_quant(sw_col):
        def fn(j, wt, tern):
            tf = F[:, F_CT:F_CT + 1024]
            P.do("vector", lambda v, wt=wt, tf=tf: v.tensor_scalar(
                tf, wt, mwb[:, sw_col:sw_col + 1], None, OP.mult), inc=False)
            last = None
            for h in range(2):
                rap = emit_round(tf[:, 512 * h:512 * h + 512])
                P.do("vector", lambda v, rap=rap: v.tensor_scalar(
                    rap, rap, 1.0 if SIM_SERIAL else 1, -1.0 if SIM_SERIAL else -1,
                    OP.min, OP.max), inc=False)
                last = P.do("vector", lambda v, tern=tern, h=h, rap=rap: v.tensor_copy(
                    tern[:, 512 * h:512 * h + 512], rap))
            return last
        return fn

    def attn_quant(inv_off, is_v):
        def fn(j, wt, tern):
            tf = F[:, F_CT:F_CT + 1024]
            P.do("vector", lambda v, wt=wt, j=j, tf=tf: v.tensor_scalar(
                tf, wt, F[:, ATT_INV + inv_off + j:ATT_INV + inv_off + j + 1], None,
                OP.mult), inc=False)
            last = None
            for h in range(2):
                rap = emit_round(tf[:, 512 * h:512 * h + 512])
                if is_v:
                    last = P.do("vector", lambda v, tern=tern, h=h, j=j, rap=rap: v.tensor_scalar(
                        tern[:, 512 * h:512 * h + 512], rap,
                        F[:, ATT_INV + 24 + j:ATT_INV + 24 + j + 1], None, OP.mult))
                else:
                    last = P.do("vector", lambda v, tern=tern, h=h, rap=rap: v.tensor_copy(
                        tern[:, 512 * h:512 * h + 512], rap))
            return last
        return fn

    def w2_dst(piece):
        i, qq = piece // 4, piece % 4
        h2, im = i // 4, i % 4
        off = 16384 * h2 + 512 * 8 * qq + 128 * im
        return bass.AP(A, P_STG + off, [[A_TOT, 128], [512, 8], [1, 128]])

    def wv_dst(j):
        return bass.AP(A, P_STG + 128 * j, [[A_TOT, 128], [1024, 8], [1, 128]])

    p2_generic(w1_src, 32, tern_quant(2),
               lambda j: A[:, P_STG + 1024 * j:P_STG + 1024 * j + 1024], scr_w1, 32768, None)
    p2_generic(w2_src, 32, tern_quant(3), w2_dst, scr_w2, 32768, None)
    p2_generic(wq_src, 8, attn_quant(0, False),
               lambda j: A[:, P_STG + 1024 * j:P_STG + 1024 * j + 1024], scr_wq, 8192, None)
    p2_generic(wk_src, 8, attn_quant(8, False),
               lambda j: A[:, P_STG + 1024 * j:P_STG + 1024 * j + 1024], scr_wk, 8192, None)
    p2_generic(wv_src, 8, attn_quant(16, True), wv_dst, scr_wv, 8192, None)

    P.wait("vector", q_sm, q_sm.v)
    P.do("vector", lambda v: v.tensor_scalar(cwp[:, :], cwp[:, :], mwb[:, 0:1], None, OP.mult))
    P.barrier(dma_sems=(q_scr, q_sm, q_bv))
    def dov(fn):
        P.do("vector", fn)
        P.selfwait("vector")

    def ln_stats_chunks(for_ffn, lo, hi, blk_base):
        for c in range(lo, hi):
            xc = x_sb[:, 1024 * c:1024 * c + 1024]
            off = blk_base + 12 * (c - lo) if blk_base is not None else F_BNB + 12 * c
            P.do("vector", lambda v, xc=xc, off=off: v.bn_stats(
                F[:, off:off + 6], xc[:, 0:512]), inc=False)
            last = (c == hi - 1) and not for_ffn
            P.do("vector", lambda v, xc=xc, off=off: v.bn_stats(
                F[:, off + 6:off + 12], xc[:, 512:1024]), inc=last)
            if for_ffn:
                P.do("vector", lambda v, xc=xc, c=c: v.tensor_reduce(
                    F[:, F_MX + c:F_MX + c + 1], xc, AX.X, OP.max), inc=False)
                P.do("vector", lambda v, xc=xc, c=c: v.tensor_reduce(
                    F[:, F_MN + c:F_MN + c + 1], xc, AX.X, OP.min), inc=(c == hi - 1))

    def ln_stats(for_ffn, pre_done=0):
        ln_stats_chunks(for_ffn, pre_done, NCH, None)
        P.selfwait("vector")
        for c in range(NCH):
            base = F_MX + 12 * c if c < pre_done else F_BNB + 12 * c
            P.do("vector", lambda v, c=c, base=base: v.bn_aggr(
                F[:, F_MV + 2 * c:F_MV + 2 * c + 2], F[:, base:base + 12]),
                 inc=(c == NCH - 1))
        P.selfwait("vector")
        mv = bass.AP(F, F_MV, [[F_TOT, 128], [2, 16]])
        vv_ = bass.AP(F, F_MV + 1, [[F_TOT, 128], [2, 16]])
        dov(lambda v: v.tensor_scalar(F[:, F_VP:F_VP + 16], vv_, EPS, None, OP.add))
        P.wait("scalar", vS, vS.v)
        P.do("scalar", lambda s: s.activation(F[:, F_SQ:F_SQ + 16], F[:, F_VP:F_VP + 16], AF.Sqrt))
        P.wait("vector", sS, sS.v)
        dov(lambda v: v.reciprocal(F[:, F_R:F_R + 16], F[:, F_SQ:F_SQ + 16]))
        dov(lambda v: v.tensor_tensor(F[:, F_TA:F_TA + 16], mv, F[:, F_R:F_R + 16], OP.mult))
        dov(lambda v: v.tensor_scalar(F[:, F_MRB:F_MRB + 16], F[:, F_TA:F_TA + 16],
                                      -1.0, None, OP.mult))
        if for_ffn:
            dov(lambda v: v.tensor_tensor(F[:, F_TA:F_TA + 16],
                                          F[:, F_MX:F_MX + 16], mv, OP.subtract))
            dov(lambda v: v.tensor_tensor(F[:, F_TB:F_TB + 16], mv,
                                          F[:, F_MN:F_MN + 16], OP.subtract))
            dov(lambda v: v.tensor_tensor(F[:, F_TA:F_TA + 16], F[:, F_TA:F_TA + 16],
                                          F[:, F_TB:F_TB + 16], OP.max))
            dov(lambda v: v.tensor_tensor(F[:, F_TA:F_TA + 16], F[:, F_TA:F_TA + 16],
                                          F[:, F_R:F_R + 16], OP.mult))
            dov(lambda v: v.tensor_scalar_max(F[:, F_TA:F_TA + 16], F[:, F_TA:F_TA + 16], 1e-5))
            dov(lambda v: v.tensor_scalar(F[:, F_IS1:F_IS1 + 16], F[:, F_TA:F_TA + 16],
                                          1.0 / 127.0, None, OP.mult))
            dov(lambda v: v.reciprocal(F[:, F_TB:F_TB + 16], F[:, F_TA:F_TA + 16]))
            dov(lambda v: v.tensor_scalar(F[:, F_TB:F_TB + 16], F[:, F_TB:F_TB + 16],
                                          127.0, None, OP.mult))
            dov(lambda v: v.tensor_tensor(F[:, F_SA:F_SA + 16], F[:, F_R:F_R + 16],
                                          F[:, F_TB:F_TB + 16], OP.mult))
            dov(lambda v: v.tensor_tensor(F[:, F_BA:F_BA + 16], F[:, F_MRB:F_MRB + 16],
                                          F[:, F_TB:F_TB + 16], OP.mult))
        P.selfwait("vector")

    def make_xhatT(for_ffn, interleave=None, v_hook=None, v_sdone=None):
        sdone = {}
        tdone = {}
        emitted = []

        def emit_chunk(c):
            xc = x_sb[:, 1024 * c:1024 * c + 1024]
            late = for_ffn and c > 8
            xq = F[:, F_XQT:F_XQT + 1024] if late else F[:, F_HRAW[c % 2]:F_HRAW[c % 2] + 1024]
            n1b = F[:, F_CT:F_CT + 1024]
            P.wait("vector", tS, tdone.get(c - 1 if (late or SIM_SERIAL) else c - 2, 0))
            if for_ffn:
                P.do("vector", lambda v, xc=xc, c=c, n1b=n1b: v.tensor_scalar(
                    n1b, xc, F[:, F_SA + c:F_SA + c + 1],
                    F[:, F_BA + c:F_BA + c + 1], OP.mult, OP.add), inc=False)
                t = None
                for h in range(2):
                    rap = emit_round(n1b[:, 512 * h:512 * h + 512])
                    t = P.do("vector", lambda v, xq=xq, h=h, c=c, rap=rap: v.tensor_scalar(
                        xq[:, 512 * h:512 * h + 512], rap,
                        F[:, F_IS1 + c:F_IS1 + c + 1], None, OP.mult))
            else:
                t = P.do("vector", lambda v, xc=xc, c=c, xq=xq: v.tensor_scalar(
                    xq, xc, F[:, F_R + c:F_R + c + 1], F[:, F_MRB + c:F_MRB + c + 1],
                    OP.mult, OP.add))
            PB = PS_C if (late or c % 2 == 0) else PS_A
            P.wait("tensor", vS, t)
            P.wait("tensor", sS, sdone.get(c - 1 if late else c - 2, 0))
            tl = None
            for k in range(8):
                tl = P.do("tensor", lambda t_, k=k, xq=xq, PB=PB: t_.transpose(
                    PB[:, 128 * k:128 * k + 128], xq[:, 128 * k:128 * k + 128], idf[:, :]),
                    inc=(k == 7))
            tdone[c] = tl
            P.wait("scalar", tS, tl)
            dst = bass.AP(A, A_N1T + 128 * c, [[A_TOT, 128], [2048, 8], [1, 128]])
            sdone[c] = P.do("scalar", lambda s, dst=dst, PB=PB: s.activation(
                dst, PB[:, 0:1024], AF.Identity))

        for c in range(NCH):
            emit_chunk(c)
            if v_hook is not None:
                v_sdone.update(sdone)
                if c >= 2:
                    v_hook(c - 2)
                if c == NCH - 1:
                    v_hook(c - 1)
                    v_hook(c)
        return sdone

    w1_guard = [0]
    w2_guard = [0]

    def ffn_stage(seq, mid_hook=None):
        marks = []
        pre_sS, pre_vS = sS.v, vS.v
        ln_stats(for_ffn=True)
        xh_done = make_xhatT(for_ffn=True)
        for sblk in range(2):
            win_lo = 0 if sblk == 0 else 1023
            pad_col = 0 if sblk == 0 else 1025
            evac_off = 1 if sblk == 0 else 0
            P.wait("vector", sS, xh_done[8])
            P.do("vector", lambda v, pc=pad_col: v.memset(
                F[:, F_HRAW[0] + pc:F_HRAW[0] + pc + 1], 0.0), inc=False)
            P.do("vector", lambda v, pc=pad_col: v.memset(
                F[:, F_HRAW[1] + pc:F_HRAW[1] + pc + 1], 0.0))
            P.selfwait("vector")
            P.wait("tensor", sS, pre_sS if sblk == 0 else sS.v)
            P.wait("tensor", sS, xh_done[8 if sblk == 0 else NCH - 1])
            P.wait("tensor", vS, pre_vS if sblk == 0 else vS.v)
            mm_done, evac_done, conv_done, silu_done = {}, {}, {}, {}
            w1fill = {}
            widths = (512, 512, 1)
            offs = (0, 512, 1024)
            P.wait("sync", tS, w1_guard[0])
            for g in (0, 1):
                w1fill[g] = P.dma("sync", A[:, A_W1S + 2048 * (g % 2):A_W1S + 2048 * (g % 2) + 2048],
                                  scr_w1[:, 2048 * g:2048 * g + 2048], q_w1[g % 2])
            for cj in range(NCJ):
                g, gi = cj // 2, cj % 2
                par = g % 2
                if gi == 0 and g >= 1 and g + 1 < 16:
                    P.wait("sync", tS, mm_done[2 * (g - 1) + 1])
                    w1fill[g + 1] = P.dma(
                        "sync", A[:, A_W1S + 2048 * ((g + 1) % 2):A_W1S + 2048 * ((g + 1) % 2) + 2048],
                        scr_w1[:, 2048 * (g + 1):2048 * (g + 1) + 2048], q_w1[(g + 1) % 2])
                w1sl = A[:, A_W1S + 2048 * par + 1024 * gi:A_W1S + 2048 * par + 1024 * gi + 1024]
                PS = PS_A if cj % 2 == 0 else PS_B
                subs = (PS[:, 0:512], PS[:, 512:1024], PS[:, 1024:1025])
                P.wait("tensor", q_w1[par], w1fill[g])
                P.wait("tensor", sS, evac_done.get(cj - 2, 0))
                last = None
                for k in range(8):
                    stat = w1sl[:, 128 * k:128 * k + 128]
                    for si in range(3):
                        mv_ap = A[:, A_N1T + 2048 * k + win_lo + offs[si]:
                                  A_N1T + 2048 * k + win_lo + offs[si] + widths[si]]
                        last = P.do("tensor", lambda t_, sub=subs[si], stat=stat, mv=mv_ap,
                                    st=(k == 0), sp=(k == 7): t_.matmul(
                            sub, stat, mv, start=st, stop=sp),
                            inc=(k == 7 and si == 2))
                mm_done[cj] = last
                if cj == NCJ - 1:
                    w1_guard[0] = last
                hb = F_HRAW[cj % 2]
                P.wait("scalar", tS, last)
                P.wait("scalar", vS, conv_done.get(cj - 2, 0))
                ed = P.do("scalar", lambda s, PS=PS, hb=hb, o=evac_off: s.activation(
                    F[:, hb + o:hb + o + 1025], PS[:, 0:1025], AF.Identity))
                evac_done[cj] = ed
                ct = F[:, F_CT:F_CT + 512 * (cj % 2) + 0:] if False else \
                     (F[:, F_CT:F_CT + 1024] if cj % 2 == 0 else F[:, F_XQT:F_XQT + 1024])
                P.wait("vector", sS, ed)
                P.do("vector", lambda v, hb=hb, cj=cj, ct=ct: v.tensor_scalar(
                    ct, F[:, hb:hb + 1024], cwp[:, cj:cj + 1], None, OP.mult), inc=False)
                P.do("vector", lambda v, hb=hb, cj=cj, ct=ct: v.scalar_tensor_tensor(
                    ct, F[:, hb + 1:hb + 1025], cwp[:, 32 + cj:32 + cj + 1], ct,
                    OP.mult, OP.add), inc=False)
                conv_done[cj] = P.do("vector", lambda v, hb=hb, cj=cj, ct=ct: v.scalar_tensor_tensor(
                    ct, F[:, hb + 2:hb + 1026], cwp[:, 64 + cj:64 + cj + 1], ct,
                    OP.mult, OP.add))
                def emit_silu(pj):
                    P.wait("scalar", vS, conv_done[pj])
                    hs = A[:, A_HS + 1024 * pj:A_HS + 1024 * pj + 1024]
                    cts = F[:, F_CT:F_CT + 1024] if pj % 2 == 0 else F[:, F_XQT:F_XQT + 1024]
                    silu_done[pj] = P.do("scalar", lambda s, hs=hs, pj=pj, cts=cts: s.activation(
                        hs, cts, AF.Silu, bias=cbp[:, pj:pj + 1]))
                if cj >= 1:
                    emit_silu(cj - 1)
                if cj >= 2:
                    pj = cj - 2
                    P.wait("vector", sS, silu_done[pj])
                    if pj == 0:
                        P.do("vector", lambda v: v.tensor_copy(
                            A[:, A_M:A_M + 1024], A[:, A_HS:A_HS + 1024]))
                    else:
                        P.do("vector", lambda v, pj=pj: v.tensor_tensor(
                            A[:, A_M:A_M + 1024], A[:, A_M:A_M + 1024],
                            A[:, A_HS + 1024 * pj:A_HS + 1024 * pj + 1024], OP.max))
            emit_silu(NCJ - 1)
            for pj in (NCJ - 2, NCJ - 1):
                P.wait("vector", sS, silu_done[pj])
                P.do("vector", lambda v, pj=pj: v.tensor_tensor(
                    A[:, A_M:A_M + 1024], A[:, A_M:A_M + 1024],
                    A[:, A_HS + 1024 * pj:A_HS + 1024 * pj + 1024], OP.max))
            P.selfwait("vector")
            if ffn_stop == "hsilu":
                P.barrier()
                continue
            P.do("vector", lambda v: v.tensor_copy(F[:, F_CT:F_CT + 1024], A[:, A_M:A_M + 1024]))
            P.selfwait("vector")
            P.wait("tensor", vS, vS.v)
            P.wait("tensor", sS, sS.v)
            tl = None
            for k in range(8):
                tl = P.do("tensor", lambda t_, k=k: t_.transpose(
                    PS_C[:, 128 * k:128 * k + 128], F[:, F_CT + 128 * k:F_CT + 128 * k + 128],
                    idf[:, :]), inc=(k == 7))
            P.wait("vector", tS, tl)
            for k in range(8):
                P.do("vector", lambda v, k=k: v.tensor_reduce(
                    F[:, F_MX2 + k:F_MX2 + k + 1], PS_C[:, 128 * k:128 * k + 128], AX.X, OP.max),
                    inc=(k == 7))
            P.selfwait("vector")
            dov(lambda v: v.tensor_scalar_max(F[:, F_MX2:F_MX2 + 8], F[:, F_MX2:F_MX2 + 8], 1e-5))
            dov(lambda v: v.reciprocal(F[:, F_S2:F_S2 + 8], F[:, F_MX2:F_MX2 + 8]))
            dov(lambda v: v.tensor_scalar(F[:, F_S2:F_S2 + 8], F[:, F_S2:F_S2 + 8],
                                          127.0, None, OP.mult))
            dov(lambda v: v.tensor_copy(A[0:128, A_S2BC:A_S2BC + 8], F[:, F_S2:F_S2 + 8]))
            dov(lambda v: v.tensor_copy(F[:, F_S2P:F_S2P + 8], A[0:128, A_S2BC:A_S2BC + 8]))
            dov(lambda v: v.reciprocal(F[:, F_ESC:F_ESC + 8], F[:, F_S2P:F_S2P + 8]))
            dov(lambda v: v.tensor_scalar(F[:, F_ESC:F_ESC + 8], F[:, F_ESC:F_ESC + 8],
                                          mwb[:, 1:2], None, OP.mult))
            P.wait("tensor", vS, vS.v)
            tl = None
            for k in range(8):
                tl = P.do("tensor", lambda t_, k=k: t_.transpose(
                    PS_A[0:1, 128 * k:128 * k + 128], F[:, F_S2P + k:F_S2P + k + 1],
                    idf[:, :]), inc=(k == 7))
            P.wait("scalar", tS, tl)
            P.do("scalar", lambda s: s.activation(A[0:1, A_S2R:A_S2R + 1024], PS_A[0:1, 0:1024],
                                                  AF.Identity))
            P.wait("tensor", sS, sS.v)
            tl = None
            for half in range(2):
                tl = P.do("tensor", lambda t_, half=half: t_.matmul(
                    PS_B[:, 512 * half:512 * half + 512], onesb[0:1, :],
                    A[0:1, A_S2R + 512 * half:A_S2R + 512 * half + 512],
                    start=True, stop=True), inc=(half == 1))
            P.wait("scalar", tS, tl)
            P.do("scalar", lambda s: s.activation(A[:, A_S2BC:A_S2BC + 1024], PS_B[:, 0:1024],
                                                  AF.Identity))
            P.wait("vector", sS, sS.v)
            P.wait("tensor", vS, vS.v)
            P.wait("tensor", sS, sS.v)
            n2done = {}
            oct_ps = [PS_A[:, 0:512], PS_A[:, 512:1024], PS_A[:, 1024:1536],
                      PS_B[:, 0:512], PS_B[:, 512:1024], PS_B[:, 1024:1536],
                      PS_C[:, 0:512], PS_C[:, 512:1024]]
            slot_guard = {0: w2_guard[0], 1: w2_guard[0]}
            for h2 in range(2):
                fills = {}
                for g in (0, 1):
                    P.wait("sync", tS, slot_guard[g % 2])
                    fills[g] = P.dma(
                        "sync", A[:, A_W2S + 2048 * (g % 2):A_W2S + 2048 * (g % 2) + 2048],
                        scr_w2[:, 16384 * h2 + 2048 * g:16384 * h2 + 2048 * g + 2048],
                        q_w2[g % 2])
                grp_last = {}
                for cj in range(NCJ):
                    g, gi = cj // 4, cj % 4
                    par = g % 2
                    if gi == 0 and g >= 1 and g + 1 < 8:
                        P.wait("sync", tS, grp_last[g - 1])
                        fills[g + 1] = P.dma(
                            "sync",
                            A[:, A_W2S + 2048 * ((g + 1) % 2):A_W2S + 2048 * ((g + 1) % 2) + 2048],
                            scr_w2[:, 16384 * h2 + 2048 * (g + 1):16384 * h2 + 2048 * (g + 1) + 2048],
                            q_w2[(g + 1) % 2])
                    if h2 == 0:
                        for h in range(2):
                            if SIM_SERIAL:
                                P.do("vector", lambda v, cj=cj, h=h: v.tensor_tensor(
                                    F[:, F_HRAW[0]:F_HRAW[0] + 512],
                                    A[:, A_HS + 1024 * cj + 512 * h:A_HS + 1024 * cj + 512 * h + 512],
                                    A[:, A_S2BC + 512 * h:A_S2BC + 512 * h + 512], OP.mult))
                                rap = emit_round(F[:, F_HRAW[0]:F_HRAW[0] + 512])
                            else:
                                P.do("vector", lambda v, cj=cj, h=h: v.tensor_tensor(
                                    sint[:, :],
                                    A[:, A_HS + 1024 * cj + 512 * h:A_HS + 1024 * cj + 512 * h + 512],
                                    A[:, A_S2BC + 512 * h:A_S2BC + 512 * h + 512], OP.mult),
                                    inc=False)
                                rap = sint[:, :]
                            n2done[cj] = P.do("vector", lambda v, cj=cj, h=h, rap=rap: v.tensor_copy(
                                A[:, A_HS + 1024 * cj + 512 * h:A_HS + 1024 * cj + 512 * h + 512],
                                rap))
                        P.wait("tensor", vS, n2done[cj])
                    w2sl = A[:, A_W2S + 2048 * par + 512 * gi:A_W2S + 2048 * par + 512 * gi + 512]
                    P.wait("tensor", q_w2[par], fills[g])
                    last = None
                    for tt in range(8):
                        pp = oct_ps[tt]
                        last = P.do("tensor", lambda t_, cj=cj, tt=tt, pp=pp, w2sl=w2sl: t_.matmul(
                            pp, A[:, A_HS + 1024 * cj + 128 * tt:A_HS + 1024 * cj + 128 * tt + 128],
                            w2sl, start=(cj == 0), stop=(cj == NCJ - 1)), inc=(tt == 7))
                    if gi == 3:
                        grp_last[g] = last
                slot_guard[0] = slot_guard[1] = last
                w2_guard[0] = last
                P.wait("vector", tS, last)
                for tt in range(8):
                    c_glob = 8 * sblk + tt
                    pp = oct_ps[tt]
                    xsl = x_sb[:, 1024 * c_glob + 512 * h2:1024 * c_glob + 512 * h2 + 512]
                    P.do("vector", lambda v, pp=pp, xsl=xsl, tt=tt: v.scalar_tensor_tensor(
                        xsl, pp, F[:, F_ESC + tt:F_ESC + tt + 1], xsl, OP.mult, OP.add),
                        inc=(tt == 7))
                P.wait("tensor", vS, vS.v)
            marks.append(vS.v)
            if sblk == 0 and mid_hook is not None:
                mid_hook()
        return marks

    def attn_stage(seq, pre_done=0):
        P.wait("sync", tS, tS.v)
        P.dma("sync", A[:, 37120:37120 + 8192], scr_wv[:, :], q_wv)
        ln_stats(for_ffn=False, pre_done=pre_done)
        evacd = {}
        on_done = {}
        sdone_holder = {}

        def emit_v(tt):
            pv = PS_B
            P.wait("tensor", q_wv, q_wv.v)
            P.wait("tensor", sS, max(evacd.get(tt - 1, 0), sdone_holder[tt]))
            last = None
            for k in range(8):
                for half in range(2):
                    P.do("tensor", lambda t_, k=k, half=half, tt=tt, pv=pv: t_.matmul(
                        pv[:, 512 * half:512 * half + 512],
                        A[:, A_N1T + 2048 * k + 128 * tt:A_N1T + 2048 * k + 128 * tt + 128],
                        A[:, 37120 + 1024 * k + 512 * half:37120 + 1024 * k + 512 * half + 512],
                        start=(k == 0), stop=False), inc=False)
            for half in range(2):
                last = P.do("tensor", lambda t_, half=half, pv=pv: t_.matmul(
                    pv[:, 512 * half:512 * half + 512], onesb[0:1, :],
                    bvb[0:1, 512 * half:512 * half + 512], start=False, stop=True),
                    inc=(half == 1))
            P.wait("scalar", tS, last)
            P.wait("scalar", vS, on_done.get(tt - 1, 0))
            vg = r2(A, AT_VG + 1040 * tt, 1040, 65)
            evacd[tt] = P.do("scalar", lambda s, vg=vg, pv=pv: s.activation(
                vg[:, :, 0:64], r2(pv, 0, 1024, 64), AF.Identity))
            P.wait("vector", sS, evacd[tt])
            on_done[tt] = P.do("vector", lambda v, vg=vg: v.memset(vg[:, :, 64:65], 1.0))

        def chase_v(c):
            sdone_holder.update(xh_partial())
            if c >= 1:
                emit_v(c - 1)
            if c == NCH - 1:
                emit_v(c)

        xh_ref = [None]

        def xh_partial():
            return xh_ref[0] if xh_ref[0] is not None else {}

        xh_ref[0] = sdone_cap = {}
        def hook(c):
            sdone_cap.update({c: None})
            chase_v(c)

        xh_done = make_xhatT(for_ffn=False, interleave=None, v_hook=emit_v,
                             v_sdone=sdone_holder)
        P.wait("tensor", sS, sS.v)
        P.wait("tensor", vS, vS.v)
        attn_pend = [None]
        prev_ce = [0]
        prev_expd = [{}]
        qkfill = {}
        P.wait("sync", tS, tS.v)
        for hp in (0, 1):
            par = hp % 2
            P.dma("sync", A[:, AT_WQS + 2048 * par:AT_WQS + 2048 * par + 1024],
                  scr_wq[:, 1024 * hp:1024 * hp + 1024], q_qk[par])
            qkfill[hp] = P.dma("sync", A[:, AT_WQS + 2048 * par + 1024:AT_WQS + 2048 * par + 2048],
                               scr_wk[:, 1024 * hp:1024 * hp + 1024], q_qk[par])
        for hp in range(8):
            par = hp % 2
            P.wait("tensor", q_qk[par], qkfill[hp])
            P.wait("tensor", vS, vS.v)
            pevac = {}
            for which, dstc, scale_t, bias_t in ((0, AT_Q, scq, bqs), (1, AT_K, sck, bks)):
                wsl = A[:, AT_WQS + 2048 * par + 1024 * which:
                        AT_WQS + 2048 * par + 1024 * which + 1024]
                for b in range(4):
                    g_lin = 4 * which + b
                    ph = PS_C[:, 512 * (g_lin % 2):512 * (g_lin % 2) + 512]
                    P.wait("tensor", sS, pevac.get(g_lin - 2, 0))
                    last = None
                    for k in range(8):
                        last = P.do("tensor", lambda t_, k=k, b=b, ph=ph, wsl=wsl: t_.matmul(
                            ph, wsl[:, 128 * k:128 * k + 128],
                            A[:, A_N1T + 2048 * k + 512 * b:A_N1T + 2048 * k + 512 * b + 512],
                            start=(k == 0), stop=(k == 7)), inc=(k == 7))
                    P.wait("scalar", tS, last)
                    pevac[g_lin] = P.do("scalar", lambda s, ph=ph, b=b, dstc=dstc,
                                             scale_t=scale_t, bias_t=bias_t, hp=hp: s.activation(
                        A[:, dstc + 512 * b:dstc + 512 * b + 512], ph, AF.Identity,
                        bias=bias_t[:, hp:hp + 1], scale=scale_t[:, hp:hp + 1]))
            if hp + 2 < 8:
                P.wait("sync", tS, tS.v)
                P.dma("sync", A[:, AT_WQS + 2048 * par:AT_WQS + 2048 * par + 1024],
                      scr_wq[:, 1024 * (hp + 2):1024 * (hp + 2) + 1024], q_qk[par])
                qkfill[hp + 2] = P.dma(
                    "sync", A[:, AT_WQS + 2048 * par + 1024:AT_WQS + 2048 * par + 2048],
                    scr_wk[:, 1024 * (hp + 2):1024 * (hp + 2) + 1024], q_qk[par])
            P.wait("tensor", sS, sS.v)
            pend = attn_pend[0]

            def ctx_ps(hi):
                return (PS_A if hi == 0 else PS_B)[0:65, 1024:1536]

            def emit_finalize():
                ph, pb, ce1 = attn_pend[0]
                attn_pend[0] = None
                P.wait("tensor", sS, sS.v)
                P.wait("tensor", vS, max(vS.v if False else ce1, ce1))
                P.wait("tensor", vS, vS.v)
                tl = None
                for hi in range(2):
                    base = F_CTXA if hi == 0 else F_CTXS
                    pdst = 0 if hi == 0 else 512
                    for g in range(4):
                        tl = P.do("tensor", lambda t_, g=g, base=base, pdst=pdst: t_.transpose(
                            PS_C[:, pdst + 65 * g:pdst + 65 * g + 65],
                            F[0:65, base + 128 * g:base + 128 * g + 128],
                            idf[0:65, 0:65]), inc=(hi == 1 and g == 3))
                P.wait("vector", tS, tl)
                for hi in range(2):
                    pdst = 0 if hi == 0 else 512
                    sums = bass.AP(PS_C, pdst + 64, [[1024, 128], [65, 4]])
                    P.do("vector", lambda v, sums=sums, hi=hi: v.reciprocal(
                        F[:, F_RV + 4 * hi:F_RV + 4 * hi + 4], sums), inc=(hi == 1))
                P.selfwait("vector")
                for hi in range(2):
                    pdst = 0 if hi == 0 else 512
                    for g in range(4):
                        c_glob = 4 * pb + g
                        xsl = x_sb[:, 1024 * c_glob + 64 * (2 * ph + hi):
                                   1024 * c_glob + 64 * (2 * ph + hi) + 64]
                        P.do("vector", lambda v, g=g, xsl=xsl, pdst=pdst, hi=hi:
                             v.scalar_tensor_tensor(
                                 xsl, PS_C[:, pdst + 65 * g:pdst + 65 * g + 64],
                                 F[:, F_RV + 4 * hi + g:F_RV + 4 * hi + g + 1],
                                 xsl, OP.mult, OP.add), inc=(hi == 1 and g == 3))
                return vS.v

            for blk in range(4):
                expd = dict(prev_expd[0])
                ctx_last = {}
                for tk in range(NCH):
                    if tk == 2 and attn_pend[0] is not None:
                        emit_finalize()
                    pp = PS_A if tk % 2 == 0 else PS_B
                    P.wait("tensor", sS, expd.get(tk - 2, 0))
                    P.do("tensor", lambda t_, tk=tk, blk=blk, pp=pp: t_.matmul(
                        pp[:, 0:512], A[0:64, AT_K + 128 * tk:AT_K + 128 * tk + 128],
                        A[0:64, AT_Q + 512 * blk:AT_Q + 512 * blk + 512],
                        start=True, stop=True, tile_position=(0, 0)), inc=False)
                    m1 = P.do("tensor", lambda t_, tk=tk, blk=blk, pp=pp: t_.matmul(
                        pp[:, 512:1024], A[64:128, AT_K + 128 * tk:AT_K + 128 * tk + 128],
                        A[64:128, AT_Q + 512 * blk:AT_Q + 512 * blk + 512],
                        start=True, stop=True, tile_position=(64, 0)))
                    if tk >= 1:
                        if tk == 1:
                            P.wait("tensor", vS, prev_ce[0])
                        ptp = A[:, PT_SLOT[(tk - 1) % 4]:PT_SLOT[(tk - 1) % 4] + 1024]
                        P.wait("tensor", sS, expd[tk - 1])
                        for hi in range(2):
                            vg = A[:, AT_VG + 1040 * (tk - 1) + 65 * (2 * hp + hi):
                                   AT_VG + 1040 * (tk - 1) + 65 * (2 * hp + hi) + 65]
                            ctx_last[tk - 1] = P.do("tensor", lambda t_, vg=vg, hi=hi, ptp=ptp,
                                                    pt_=tk - 1: t_.matmul(
                                ctx_ps(hi), vg, ptp[:, 512 * hi:512 * hi + 512],
                                start=(pt_ == 0), stop=(pt_ == NCH - 1)), inc=(hi == 1))
                    pt = A[:, PT_SLOT[tk % 4]:PT_SLOT[tk % 4] + 1024]
                    P.wait("scalar", tS, m1)
                    if tk >= 4:
                        P.wait("scalar", tS, ctx_last[tk - 4])
                    expd[tk] = P.do("scalar", lambda s, pp=pp, pt=pt: s.activation(
                        pt, pp[:, 0:1024], AF.Exp, scale=0.125))
                ptp = A[:, PT_SLOT[(NCH - 1) % 4]:PT_SLOT[(NCH - 1) % 4] + 1024]
                P.wait("tensor", sS, expd[NCH - 1])
                for hi in range(2):
                    vg = A[:, AT_VG + 1040 * (NCH - 1) + 65 * (2 * hp + hi):
                           AT_VG + 1040 * (NCH - 1) + 65 * (2 * hp + hi) + 65]
                    ctx_last[NCH - 1] = P.do("tensor", lambda t_, vg=vg, hi=hi, ptp=ptp: t_.matmul(
                        ctx_ps(hi), vg, ptp[:, 512 * hi:512 * hi + 512],
                        start=False, stop=True), inc=(hi == 1))
                ce1 = None
                for hi in range(2):
                    P.wait("vector", tS, ctx_last[NCH - 1])
                    base = F_CTXA if hi == 0 else F_CTXS
                    ce1 = P.do("vector", lambda v, hi=hi, base=base: v.tensor_copy(
                        F[0:65, base:base + 512], ctx_ps(hi)))
                attn_pend[0] = (hp, blk, ce1)
                prev_ce[0] = ce1
                prev_expd[0] = {14 - 16: expd[14], 15 - 16: expd[15]}
        if attn_pend[0] is not None:
            emit_finalize()
        w1_guard[0] = max(w1_guard[0], tS.v)

    steps = [(0, "A"), (0, "T"), (0, "C"), (1, "A"), (1, "T"), (1, "C")][:n_steps]
    loaded = set()
    if n_steps == 0:
        src0 = bass.AP(xin, 0, [[D, 128], [128 * D, NCH], [1, D]])
        P.dma("gpsimd", bass.AP(x_sb, 0, [[NCH * D, 128], [D, NCH], [1, D]]), src0, q_x[0])
        for e in ENGS:
            P.wait(e, q_x[0], q_x[0].v)
        loaded.add(0)
    for si, (seq, st) in enumerate(steps):
        if seq not in loaded:
            if seq == 1:
                for e in ENGS:
                    P.wait(e, q_out[0], q_out[0].v)
            src = bass.AP(xin, seq * S * D, [[D, 128], [128 * D, NCH], [1, D]])
            P.dma("gpsimd", bass.AP(x_sb, 0, [[NCH * D, 128], [D, NCH], [1, D]]), src, q_x[seq])
            for e in ENGS:
                P.wait(e, q_x[seq], q_x[seq].v)
            loaded.add(seq)
        marks = None
        if st in ("A", "C"):
            nxt = steps[si + 1][1] if si + 1 < len(steps) else None
            hook = None
            if nxt == "T":
                def hook():
                    ln_stats_chunks(False, 0, 8, F_MX)
            marks = ffn_stage(seq, mid_hook=hook)
        else:
            attn_stage(seq, pre_done=0 if steps[si - 1][1] not in ("A",) else 8)
        last_of_seq = (si == len(steps) - 1) or (steps[si + 1][0] != seq)
        if last_of_seq:
            if marks is not None and len(marks) == 2:
                for half, mk in enumerate(marks):
                    P.wait("gpsimd", vS, mk)
                    dst = bass.AP(xout, seq * S * D + half * 8 * 128 * D,
                                  [[D, 128], [128 * D, 8], [1, D]])
                    srcp = bass.AP(x_sb, half * 8 * D, [[NCH * D, 128], [D, 8], [1, D]])
                    P.dma("gpsimd", dst, srcp, q_out[seq])
            else:
                P.barrier()
                dst = bass.AP(xout, seq * S * D, [[D, 128], [128 * D, NCH], [1, D]])
                P.dma("gpsimd", dst, bass.AP(x_sb, 0, [[NCH * D, 128], [D, NCH], [1, D]]),
                      q_out[seq])
    if dump is not None or n_steps == 0:
        P.barrier()
        for s_ in (q_out[0], q_out[1]):
            P.wait("gpsimd", s_, s_.v)
        if dump is None:
            dst = bass.AP(xout, 0, [[D, 128], [128 * D, NCH], [1, D]])
            P.dma("gpsimd", dst, bass.AP(x_sb, 0, [[NCH * D, 128], [D, NCH], [1, D]]), q_out[0])
        else:
            which, lo, ncols = dump
            tsrc = {"A": A, "F": F, "X": x_sb}[which]
            dst = bass.AP(xout, 0, [[ncols, 128], [1, ncols]])
            P.dma("gpsimd", dst, tsrc[:, lo:lo + ncols], q_out[0])
    for seq in (0, 1):
        if q_out[seq].v:
            P.wait("gpsimd", q_out[seq], q_out[seq].v)

    with nc.Block() as block:
        @block.sync
        def _(eng):
            for fn in P.ops["sync"]:
                fn(eng)

        @block.tensor
        def _(eng):
            for fn in P.ops["tensor"]:
                fn(eng)

        @block.vector
        def _(eng):
            for fn in P.ops["vector"]:
                fn(eng)

        @block.scalar
        def _(eng):
            for fn in P.ops["scalar"]:
                fn(eng)

        @block.gpsimd
        def _(eng):
            for fn in P.ops["gpsimd"]:
                fn(eng)

    es.close()
    return nc


def kernel(L0_attn_hidden_states, L0_attn_Wq, L0_attn_bq, L0_attn_Wk, L0_attn_bk,
           L0_attn_Wv, L0_attn_bv, L0_ffn_w1, L0_ffn_conv_w, L0_ffn_conv_b, L0_ffn_w2,
           _n_steps=6):
    x = np.ascontiguousarray(L0_attn_hidden_states, dtype=np.float32)
    nc = build_program(_n_steps)
    cf = lambda a: np.ascontiguousarray(a, dtype=np.float32)
    in_maps = []
    for c in range(8):
        in_maps.append({
            "xin": cf(x[2 * c:2 * c + 2]),
            "wq": cf(L0_attn_Wq), "wk": cf(L0_attn_Wk), "wv": cf(L0_attn_Wv),
            "bq": cf(L0_attn_bq), "bk": cf(L0_attn_bk), "bv": cf(L0_attn_bv),
            "w1": cf(L0_ffn_w1), "w2": cf(L0_ffn_w2),
            "cw": cf(L0_ffn_conv_w), "cb": cf(L0_ffn_conv_b),
        })
    res = run_bass_kernel_spmd(nc, in_maps, core_ids=list(range(8)))
    out = np.concatenate([r["xout"] for r in res.results], axis=0)
    return out


# version 23
# speedup vs baseline: 1.0328x; 1.0269x over previous
import os
import numpy as np
from contextlib import ExitStack
import concourse.bass as bass
import concourse.mybir as mybir
from concourse.bass_utils import run_bass_kernel_spmd

F32 = mybir.dt.float32
BF16 = mybir.dt.bfloat16
AF = mybir.ActivationFunctionType
OP = mybir.AluOpType
AX = mybir.AxisListType

MAGIC = 3 * 2.0 ** 22
MAGIC64 = 3 * 2.0 ** 51
D = 1024
S = 2048
NCH = 16
DH = 4096
NCJ = 32
EPS = 1e-5

A_N1T = 0
A_HS = 16384
A_W1S = 49152
A_W2S = 53248
A_S2BC = 57344
A_M = 58368
A_S2R = 58368
A_TOT = 59392
AT_VG = 16384
AT_Q = 33024
AT_K = 35072
AT_PT = 37120
AT_WQS = 39168
AT_WVS = 43264
PT_SLOT = (37120, 38144, 49152, 50176)
P_STG = 0
P_TERN = 32768

F_HRAW = (0, 1026)
F_CT = 2052
F_BN = 3076
F_MV = 3088
F_MX = 3120
F_MN = 3136
F_R = 3152
F_MRB = 3168
F_SA = 3184
F_BA = 3200
F_IS1 = 3216
F_TA = 3232
F_TB = 3248
F_MX2 = 3264
F_S2 = 3272
F_S2P = 3280
F_ESC = 3288
F_RV = 3296
F_SQ = 3300
F_VP = 3316
F_W = 3332
F_SCQ = 3364
F_ONE = 3372
F_WT = 3392
F_BNB = 3392
F_CTXS = 4416
F_CTXA = 3584
F_XQT = 3392
F_TOT = 5440

ENGS = ("sync", "tensor", "vector", "scalar", "gpsimd")
SIM_SERIAL = os.environ.get("BASS_SIM_FP64") == "1"


class Sem:
    __slots__ = ("h", "v", "name")

    def __init__(self, nc, name):
        self.h = nc.alloc_semaphore(name=name)
        self.v = 0
        self.name = name


class Prog:
    def __init__(self, nc):
        self.nc = nc
        self.ops = {e: [] for e in ENGS}
        self.es = {e: Sem(nc, "es_" + e) for e in ENGS}
        self._lastwait = {}

    def do(self, e, fn, inc=True):
        if SIM_SERIAL:
            s = self.es[e]
            s.v += 1
            self.ops[e].append(lambda eng, f=fn, h=s.h: f(eng).then_inc(h, 1))
            self.ops[e].append(lambda eng, h=s.h, v=s.v: eng.wait_ge(h, v))
            self._lastwait[(e, s.name)] = s.v
            return s.v
        if inc:
            s = self.es[e]
            s.v += 1
            self.ops[e].append(lambda eng, f=fn, h=s.h: f(eng).then_inc(h, 1))
            return s.v
        self.ops[e].append(fn)
        return None

    def dma(self, e, out, in_, dsem, slow=False):
        dsem.v += 16
        self.ops[e].append(
            lambda eng, o=out, i=in_, h=dsem.h, sl=slow: eng.dma_start(
                out=o, in_=i, allow_slow_non_contiguous=sl).then_inc(h, 16))
        return dsem.v

    def wait(self, e, sem, val):
        if not val:
            return
        key = (e, sem.name)
        if self._lastwait.get(key, -1) >= val:
            return
        self._lastwait[key] = val
        self.ops[e].append(lambda eng, h=sem.h, v=val: eng.wait_ge(h, v))

    def selfwait(self, e):
        self.wait(e, self.es[e], self.es[e].v)

    def barrier(self, dma_sems=()):
        snap = {e: self.es[e].v for e in ENGS}
        for e in ENGS:
            for o in ENGS:
                if o != e:
                    self.wait(e, self.es[o], snap[o])
            for ds in dma_sems:
                self.wait(e, ds, ds.v)


def r2(t, lo, n, inner):
    return t[:, lo:lo + n].rearrange("p (a b) -> p a b", b=inner)


def build_program(n_steps=6, dump=None, ffn_stop=None):
    nc = bass.Bass("TRN2", target_bir_lowering=False)
    es = ExitStack()
    nc._keepalive_es = es

    xin = nc.dram_tensor("xin", [2, S, D], F32, kind="ExternalInput")
    wq_d = nc.dram_tensor("wq", [D, D], F32, kind="ExternalInput")
    wk_d = nc.dram_tensor("wk", [D, D], F32, kind="ExternalInput")
    wv_d = nc.dram_tensor("wv", [D, D], F32, kind="ExternalInput")
    bq_d = nc.dram_tensor("bq", [D], F32, kind="ExternalInput")
    bk_d = nc.dram_tensor("bk", [D], F32, kind="ExternalInput")
    bv_d = nc.dram_tensor("bv", [D], F32, kind="ExternalInput")
    w1_d = nc.dram_tensor("w1", [DH, D], F32, kind="ExternalInput")
    w2_d = nc.dram_tensor("w2", [D, DH], F32, kind="ExternalInput")
    cw_d = nc.dram_tensor("cw", [DH, 1, 3], F32, kind="ExternalInput")
    cb_d = nc.dram_tensor("cb", [DH], F32, kind="ExternalInput")
    xout = nc.dram_tensor("xout", [2, S, D], F32, kind="ExternalOutput")
    scr_w1 = nc.dram_tensor("scr_w1", [128, 32768], BF16)
    scr_w2 = nc.dram_tensor("scr_w2", [128, 32768], BF16)
    scr_wq = nc.dram_tensor("scr_wq", [128, 8192], BF16)
    scr_wk = nc.dram_tensor("scr_wk", [128, 8192], BF16)
    scr_wv = nc.dram_tensor("scr_wv", [128, 8192], BF16)

    def sb(name, shape, dt):
        return es.enter_context(nc.sbuf_tensor(name, shape, dt))

    def ps(name, shape, dt):
        return es.enter_context(nc.psum_tensor(name, shape, dt))

    x_sb = sb("x_sb", [128, NCH * D], F32)
    A = sb("A", [128, A_TOT], BF16)
    F = sb("F", [128, F_TOT], F32)
    idf = sb("idf", [128, 128], F32)
    onesb = sb("onesb", [1, 128], BF16)
    onesf = sb("onesf", [1, 128], F32)
    cwp = sb("cwp", [128, 96], F32)
    cbp = sb("cbp", [128, 32], F32)
    bqs = sb("bqs", [128, 8], F32)
    bks = sb("bks", [128, 8], F32)
    scq = sb("scq", [128, 8], F32)
    sck = sb("sck", [128, 8], F32)
    bvb = sb("bvb", [1, D], BF16)
    sint = sb("sint", [128, 512], mybir.dt.int32)
    mwb = sb("mwb", [128, 4], F32)

    PS_A = ps("PS_A", [128, 1536], F32)
    PS_B = ps("PS_B", [128, 1536], F32)
    PS_C = ps("PS_C", [128, 1024], F32)

    P = Prog(nc)

    def emit_round(src_ap):
        if SIM_SERIAL:
            P.do("vector", lambda v, s=src_ap: v.tensor_scalar(
                s, s, MAGIC64, MAGIC64, OP.add, OP.subtract))
            return src_ap
        P.do("vector", lambda v, s=src_ap: v.tensor_copy(sint[:, :], s), inc=False)
        return sint[:, :]

    q_x = [Sem(nc, "q_x0"), Sem(nc, "q_x1")]
    q_out = [Sem(nc, "q_o0"), Sem(nc, "q_o1")]
    q_pw = [Sem(nc, "q_pw0"), Sem(nc, "q_pw1")]
    q_scr = Sem(nc, "q_scr")
    q_sm = Sem(nc, "q_sm")
    q_bv = Sem(nc, "q_bv")
    q_w1 = [Sem(nc, "q_w1a"), Sem(nc, "q_w1b")]
    q_w2 = [Sem(nc, "q_w2a"), Sem(nc, "q_w2b")]
    q_qk = [Sem(nc, "q_qka"), Sem(nc, "q_qkb")]
    q_wv = Sem(nc, "q_wv")

    tS, vS, sS, gS = P.es["tensor"], P.es["vector"], P.es["scalar"], P.es["gpsimd"]


    P.do("gpsimd", lambda g: g.iota(F[:, 0:128], [[1, 128]], channel_multiplier=0,
                                    allow_small_or_imprecise_dtypes=True))
    P.do("gpsimd", lambda g: g.iota(F[:, 128:256], [[0, 128]], channel_multiplier=1,
                                    allow_small_or_imprecise_dtypes=True))
    P.wait("vector", gS, gS.v)
    P.do("vector", lambda v: v.tensor_tensor(idf[:, :], F[:, 0:128], F[:, 128:256], OP.is_equal))
    P.do("vector", lambda v: v.memset(onesb[:, :], 1.0))
    P.do("vector", lambda v: v.memset(onesf[:, :], 1.0))
    P.do("vector", lambda v: v.memset(F[:, F_ONE:F_ONE + 1], 1.0))

    P.dma("sync", bqs[:, :], bass.AP(bq_d, 0, [[1, 128], [128, 8]]), q_sm, slow=True)
    P.dma("sync", bks[:, :], bass.AP(bk_d, 0, [[1, 128], [128, 8]]), q_sm, slow=True)
    P.dma("gpsimd", bvb[:, :], bass.AP(bv_d, 0, [[1024, 1], [1, 1024]]), q_bv)
    P.dma("sync", bass.AP(cwp, 0, [[96, 128], [32, 3], [1, 32]]),
          bass.AP(cw_d, 0, [[3, 128], [1, 3], [384, 32]]), q_sm, slow=True)
    P.dma("sync", cbp[:, :], bass.AP(cb_d, 0, [[1, 128], [128, 32]]), q_sm, slow=True)

    def w1_src(j):
        return bass.AP(w1_d, j * 128 * 1024, [[1024, 128], [1, 1024]])

    def w2_src(piece):
        i, qq = piece // 4, piece % 4
        return bass.AP(w2_d, i * 128 * 4096 + 1024 * qq, [[4096, 128], [1, 1024]])

    def wq_src(j):
        return bass.AP(wq_d, j * 128 * 1024, [[1024, 128], [1, 1024]])

    def wk_src(j):
        return bass.AP(wk_d, j * 128 * 1024, [[1024, 128], [1, 1024]])

    def wv_src(j):
        return bass.AP(wv_d, j * 128 * 1024, [[1024, 128], [1, 1024]])

    PREP_ST = F_MX
    p1 = [(w1_src, 32, OP.add, F_W),
          (w2_src, 32, OP.add, PREP_ST),
          (wq_src, 8, OP.max, PREP_ST + 32),
          (wk_src, 8, OP.max, PREP_ST + 40),
          (wv_src, 8, OP.max, PREP_ST + 48)]
    d1 = {}
    gi_ = 0
    p1_slots = [F[:, F_WT:F_WT + 1024], F[:, F_WT + 1024:F_WT + 2048],
                F[:, F_HRAW[0]:F_HRAW[0] + 1024], F[:, F_HRAW[1]:F_HRAW[1] + 1024],
                x_sb[:, 0:1024], x_sb[:, 1024:2048]]
    q_p1 = [q_pw[0], q_pw[1], q_w1[0], q_w1[1], q_w2[0], q_w2[1]]
    P.wait("sync", vS, vS.v)
    for srcf, np_, op_, base in p1:
        for j in range(np_):
            par = gi_ % 6
            wt = p1_slots[par]
            P.wait("sync", vS, d1.get(gi_ - 6, 0))
            P.dma("sync", wt, srcf(j), q_p1[par])
            P.wait("vector", q_p1[par], q_p1[par].v)
            d1[gi_] = P.do("vector", lambda v, wt=wt, base=base, j=j, op_=op_: v.tensor_reduce(
                F[:, base + j:base + j + 1], wt, AX.X, op_, apply_absolute_value=True))
            gi_ += 1
    pass1_v_end = vS.v

    def mean_finalize(sum_base, npieces, mw_col, sw_col):
        P.selfwait("vector")
        P.do("vector", lambda v: v.tensor_reduce(
            F[:, F_TA:F_TA + 1], F[:, sum_base:sum_base + npieces], AX.X, OP.add))
        P.selfwait("vector")
        P.wait("tensor", vS, vS.v)
        P.do("tensor", lambda t: t.matmul(PS_C[0:1, 0:1], F[:, F_TA:F_TA + 1],
                                          F[:, F_ONE:F_ONE + 1], start=True, stop=True))
        P.wait("scalar", tS, tS.v)
        P.do("scalar", lambda s: s.activation(F[0:1, F_TA:F_TA + 1], PS_C[0:1, 0:1],
                                              AF.Identity, scale=1.0 / (4096.0 * 1024.0)))
        P.wait("vector", sS, sS.v)
        P.do("vector", lambda v: v.tensor_scalar_max(
            F[0:1, F_TA:F_TA + 1], F[0:1, F_TA:F_TA + 1], 1e-5))
        P.selfwait("vector")
        P.do("vector", lambda v: v.reciprocal(F[0:1, F_TB:F_TB + 1], F[0:1, F_TA:F_TA + 1]))
        P.selfwait("vector")
        P.wait("tensor", vS, vS.v)
        P.do("tensor", lambda t: t.matmul(PS_C[:, 4:5], onesf[0:1, :], F[0:1, F_TA:F_TA + 1],
                                          start=True, stop=True))
        P.do("tensor", lambda t: t.matmul(PS_C[:, 5:6], onesf[0:1, :], F[0:1, F_TB:F_TB + 1],
                                          start=True, stop=True))
        P.wait("scalar", tS, tS.v)
        P.do("scalar", lambda s: s.activation(mwb[:, mw_col:mw_col + 1], PS_C[:, 4:5], AF.Identity,
                                              scale=(0.5 if mw_col == 1 else 1.0)))
        P.do("scalar", lambda s: s.activation(mwb[:, sw_col:sw_col + 1], PS_C[:, 5:6], AF.Identity))
        P.wait("vector", sS, sS.v)

    mean_finalize(F_W, 32, 0, 2)
    mean_finalize(PREP_ST, 32, 1, 3)
    ATT_INV = PREP_ST + 56
    for idx, (mx_base, sc_dst) in enumerate(((PREP_ST + 32, scq), (PREP_ST + 40, sck),
                                             (PREP_ST + 48, None))):
        P.do("vector", lambda v, mx_base=mx_base, idx=idx: v.tensor_scalar(
            F[:, F_TA:F_TA + 8], F[:, mx_base:mx_base + 8], 1.0 / 127.0, 1e-8,
            OP.mult, OP.max))
        P.selfwait("vector")
        if sc_dst is not None:
            P.do("vector", lambda v, sc_dst=sc_dst: v.tensor_copy(sc_dst[:, :], F[:, F_TA:F_TA + 8]))
        else:
            P.do("vector", lambda v: v.tensor_copy(F[:, ATT_INV + 24:ATT_INV + 32],
                                                   F[:, F_TA:F_TA + 8]))
        P.do("vector", lambda v, idx=idx: v.reciprocal(
            F[:, ATT_INV + 8 * idx:ATT_INV + 8 * idx + 8], F[:, F_TA:F_TA + 8]))
        P.selfwait("vector")

    p2_first = [True]
    d2v, d2s, d2t = {}, {}, {}
    g2 = [0]
    tern_par = [0]

    def p2_generic(srcf, npieces, quant_fn, evac_dst, scr, scr_cols, stage_cols):
        if p2_first[0]:
            P.wait("sync", vS, pass1_v_end)
            p2_first[0] = False
        P.wait("scalar", q_scr, q_scr.v)
        p2_slots = [F[:, F_WT:F_WT + 1024], F[:, F_WT + 1024:F_WT + 2048],
                    x_sb[:, 0:1024], x_sb[:, 1024:2048]]
        q_p2 = [q_pw[0], q_pw[1], q_w1[0], q_w1[1]]
        for j in range(npieces):
            gj = g2[0]
            g2[0] += 1
            par = gj % 2
            lpar = gj % 4
            tern_par[0] = par
            wt = p2_slots[lpar]
            P.wait("sync", vS, d2v.get(gj - 4, 0))
            P.dma("sync", wt, srcf(j), q_p2[lpar])
            P.wait("vector", q_p2[lpar], q_p2[lpar].v)
            tern = F[:, F_HRAW[par]:F_HRAW[par] + 1024]
            P.wait("vector", tS, d2t.get(gj - 1 if SIM_SERIAL else gj - 2, 0))
            d2v[gj] = quant_fn(j, wt, tern)
            PB = PS_C if par == 0 else PS_A
            P.wait("tensor", vS, d2v[gj])
            P.wait("tensor", sS, d2s.get(gj - 2, 0))
            tl = None
            for k in range(8):
                tl = P.do("tensor", lambda t_, k=k, PB=PB, tern=tern: t_.transpose(
                    PB[:, 128 * k:128 * k + 128], tern[:, 128 * k:128 * k + 128], idf[:, :]),
                    inc=(k == 7))
            d2t[gj] = tl
            P.wait("scalar", tS, tl)
            d2s[gj] = P.do("scalar", lambda s, j=j, PB=PB: s.activation(
                evac_dst(j), PB[:, 0:1024], AF.Identity))
        P.wait("gpsimd", sS, sS.v)
        P.dma("gpsimd", scr[:, 0:scr_cols], A[:, P_STG:P_STG + scr_cols], q_scr)

    def tern_quant(sw_col):
        def fn(j, wt, tern):
            tf = F[:, F_CT:F_CT + 1024]
            P.do("vector", lambda v, wt=wt, tf=tf: v.tensor_scalar(
                tf, wt, mwb[:, sw_col:sw_col + 1], None, OP.mult), inc=False)
            last = None
            for h in range(2):
                rap = emit_round(tf[:, 512 * h:512 * h + 512])
                last = P.do("vector", lambda v, tern=tern, h=h, rap=rap: v.tensor_scalar(
                    tern[:, 512 * h:512 * h + 512], rap,
                    1.0 if SIM_SERIAL else 1, -1.0 if SIM_SERIAL else -1,
                    OP.min, OP.max))
            return last
        return fn

    def attn_quant(inv_off, is_v):
        def fn(j, wt, tern):
            tf = F[:, F_CT:F_CT + 1024]
            P.do("vector", lambda v, wt=wt, j=j, tf=tf: v.tensor_scalar(
                tf, wt, F[:, ATT_INV + inv_off + j:ATT_INV + inv_off + j + 1], None,
                OP.mult), inc=False)
            last = None
            for h in range(2):
                rap = emit_round(tf[:, 512 * h:512 * h + 512])
                if is_v:
                    last = P.do("vector", lambda v, tern=tern, h=h, j=j, rap=rap: v.tensor_scalar(
                        tern[:, 512 * h:512 * h + 512], rap,
                        F[:, ATT_INV + 24 + j:ATT_INV + 24 + j + 1], None, OP.mult))
                else:
                    last = P.do("vector", lambda v, tern=tern, h=h, rap=rap: v.tensor_copy(
                        tern[:, 512 * h:512 * h + 512], rap))
            return last
        return fn

    def w2_dst(piece):
        i, qq = piece // 4, piece % 4
        h2, im = i // 4, i % 4
        off = 16384 * h2 + 512 * 8 * qq + 128 * im
        return bass.AP(A, P_STG + off, [[A_TOT, 128], [512, 8], [1, 128]])

    def wv_dst(j):
        return bass.AP(A, P_STG + 128 * j, [[A_TOT, 128], [1024, 8], [1, 128]])

    p2_generic(w1_src, 32, tern_quant(2),
               lambda j: A[:, P_STG + 1024 * j:P_STG + 1024 * j + 1024], scr_w1, 32768, None)
    p2_generic(w2_src, 32, tern_quant(3), w2_dst, scr_w2, 32768, None)
    p2_generic(wq_src, 8, attn_quant(0, False),
               lambda j: A[:, P_STG + 1024 * j:P_STG + 1024 * j + 1024], scr_wq, 8192, None)
    p2_generic(wk_src, 8, attn_quant(8, False),
               lambda j: A[:, P_STG + 1024 * j:P_STG + 1024 * j + 1024], scr_wk, 8192, None)
    p2_generic(wv_src, 8, attn_quant(16, True), wv_dst, scr_wv, 8192, None)

    P.wait("vector", q_sm, q_sm.v)
    P.do("vector", lambda v: v.tensor_scalar(cwp[:, :], cwp[:, :], mwb[:, 0:1], None, OP.mult))
    P.barrier(dma_sems=(q_scr, q_sm, q_bv))
    def dov(fn):
        P.do("vector", fn)
        P.selfwait("vector")

    def ln_stats_chunks(for_ffn, lo, hi, blk_base):
        for c in range(lo, hi):
            xc = x_sb[:, 1024 * c:1024 * c + 1024]
            off = blk_base + 12 * (c - lo) if blk_base is not None else F_BNB + 12 * c
            P.do("vector", lambda v, xc=xc, off=off: v.bn_stats(
                F[:, off:off + 6], xc[:, 0:512]), inc=False)
            last = (c == hi - 1) and not for_ffn
            P.do("vector", lambda v, xc=xc, off=off: v.bn_stats(
                F[:, off + 6:off + 12], xc[:, 512:1024]), inc=last)
            if for_ffn:
                P.do("vector", lambda v, xc=xc, c=c: v.tensor_reduce(
                    F[:, F_MX + c:F_MX + c + 1], xc, AX.X, OP.max), inc=False)
                P.do("vector", lambda v, xc=xc, c=c: v.tensor_reduce(
                    F[:, F_MN + c:F_MN + c + 1], xc, AX.X, OP.min), inc=(c == hi - 1))

    def ln_stats(for_ffn, pre_done=0):
        ln_stats_chunks(for_ffn, pre_done, NCH, None)
        P.selfwait("vector")
        for c in range(NCH):
            base = F_MX + 12 * c if c < pre_done else F_BNB + 12 * c
            P.do("vector", lambda v, c=c, base=base: v.bn_aggr(
                F[:, F_MV + 2 * c:F_MV + 2 * c + 2], F[:, base:base + 12]),
                 inc=(c == NCH - 1))
        P.selfwait("vector")
        mv = bass.AP(F, F_MV, [[F_TOT, 128], [2, 16]])
        vv_ = bass.AP(F, F_MV + 1, [[F_TOT, 128], [2, 16]])
        dov(lambda v: v.tensor_scalar(F[:, F_VP:F_VP + 16], vv_, EPS, None, OP.add))
        P.wait("scalar", vS, vS.v)
        P.do("scalar", lambda s: s.activation(F[:, F_SQ:F_SQ + 16], F[:, F_VP:F_VP + 16], AF.Sqrt))
        P.wait("vector", sS, sS.v)
        dov(lambda v: v.reciprocal(F[:, F_R:F_R + 16], F[:, F_SQ:F_SQ + 16]))
        dov(lambda v: v.tensor_tensor(F[:, F_TA:F_TA + 16], mv, F[:, F_R:F_R + 16], OP.mult))
        dov(lambda v: v.tensor_scalar(F[:, F_MRB:F_MRB + 16], F[:, F_TA:F_TA + 16],
                                      -1.0, None, OP.mult))
        if for_ffn:
            dov(lambda v: v.tensor_tensor(F[:, F_TA:F_TA + 16],
                                          F[:, F_MX:F_MX + 16], mv, OP.subtract))
            dov(lambda v: v.tensor_tensor(F[:, F_TB:F_TB + 16], mv,
                                          F[:, F_MN:F_MN + 16], OP.subtract))
            dov(lambda v: v.tensor_tensor(F[:, F_TA:F_TA + 16], F[:, F_TA:F_TA + 16],
                                          F[:, F_TB:F_TB + 16], OP.max))
            dov(lambda v: v.tensor_tensor(F[:, F_TA:F_TA + 16], F[:, F_TA:F_TA + 16],
                                          F[:, F_R:F_R + 16], OP.mult))
            dov(lambda v: v.tensor_scalar_max(F[:, F_TA:F_TA + 16], F[:, F_TA:F_TA + 16], 1e-5))
            dov(lambda v: v.tensor_scalar(F[:, F_IS1:F_IS1 + 16], F[:, F_TA:F_TA + 16],
                                          1.0 / 127.0, None, OP.mult))
            dov(lambda v: v.reciprocal(F[:, F_TB:F_TB + 16], F[:, F_TA:F_TA + 16]))
            dov(lambda v: v.tensor_scalar(F[:, F_TB:F_TB + 16], F[:, F_TB:F_TB + 16],
                                          127.0, None, OP.mult))
            dov(lambda v: v.tensor_tensor(F[:, F_SA:F_SA + 16], F[:, F_R:F_R + 16],
                                          F[:, F_TB:F_TB + 16], OP.mult))
            dov(lambda v: v.tensor_tensor(F[:, F_BA:F_BA + 16], F[:, F_MRB:F_MRB + 16],
                                          F[:, F_TB:F_TB + 16], OP.mult))
        P.selfwait("vector")

    def make_xhatT(for_ffn, interleave=None, v_hook=None, v_sdone=None):
        sdone = {}
        tdone = {}
        emitted = []

        def emit_chunk(c):
            xc = x_sb[:, 1024 * c:1024 * c + 1024]
            late = for_ffn and c > 8
            xq = F[:, F_XQT:F_XQT + 1024] if late else F[:, F_HRAW[c % 2]:F_HRAW[c % 2] + 1024]
            n1b = F[:, F_CT:F_CT + 1024]
            P.wait("vector", tS, tdone.get(c - 1 if (late or SIM_SERIAL) else c - 2, 0))
            if for_ffn:
                P.do("vector", lambda v, xc=xc, c=c, n1b=n1b: v.tensor_scalar(
                    n1b, xc, F[:, F_SA + c:F_SA + c + 1],
                    F[:, F_BA + c:F_BA + c + 1], OP.mult, OP.add), inc=False)
                t = None
                for h in range(2):
                    rap = emit_round(n1b[:, 512 * h:512 * h + 512])
                    t = P.do("vector", lambda v, xq=xq, h=h, c=c, rap=rap: v.tensor_scalar(
                        xq[:, 512 * h:512 * h + 512], rap,
                        F[:, F_IS1 + c:F_IS1 + c + 1], None, OP.mult))
            else:
                t = P.do("vector", lambda v, xc=xc, c=c, xq=xq: v.tensor_scalar(
                    xq, xc, F[:, F_R + c:F_R + c + 1], F[:, F_MRB + c:F_MRB + c + 1],
                    OP.mult, OP.add))
            PB = PS_C if (late or c % 2 == 0) else PS_A
            P.wait("tensor", vS, t)
            P.wait("tensor", sS, sdone.get(c - 1 if late else c - 2, 0))
            tl = None
            for k in range(8):
                tl = P.do("tensor", lambda t_, k=k, xq=xq, PB=PB: t_.transpose(
                    PB[:, 128 * k:128 * k + 128], xq[:, 128 * k:128 * k + 128], idf[:, :]),
                    inc=(k == 7))
            tdone[c] = tl
            P.wait("scalar", tS, tl)
            dst = bass.AP(A, A_N1T + 128 * c, [[A_TOT, 128], [2048, 8], [1, 128]])
            sdone[c] = P.do("scalar", lambda s, dst=dst, PB=PB: s.activation(
                dst, PB[:, 0:1024], AF.Identity))

        for c in range(NCH):
            emit_chunk(c)
            if v_hook is not None:
                v_sdone.update(sdone)
                if c >= 2:
                    v_hook(c - 2)
                if c == NCH - 1:
                    v_hook(c - 1)
                    v_hook(c)
        return sdone

    w1_guard = [0]
    w2_guard = [0]

    def ffn_stage(seq, mid_hook=None):
        marks = []
        pre_sS, pre_vS = sS.v, vS.v
        ln_stats(for_ffn=True)
        xh_done = make_xhatT(for_ffn=True)
        for sblk in range(2):
            win_lo = 0 if sblk == 0 else 1023
            pad_col = 0 if sblk == 0 else 1025
            evac_off = 1 if sblk == 0 else 0
            P.wait("vector", sS, xh_done[8])
            P.do("vector", lambda v, pc=pad_col: v.memset(
                F[:, F_HRAW[0] + pc:F_HRAW[0] + pc + 1], 0.0), inc=False)
            P.do("vector", lambda v, pc=pad_col: v.memset(
                F[:, F_HRAW[1] + pc:F_HRAW[1] + pc + 1], 0.0))
            P.selfwait("vector")
            P.wait("tensor", sS, pre_sS if sblk == 0 else sS.v)
            P.wait("tensor", sS, xh_done[8 if sblk == 0 else NCH - 1])
            P.wait("tensor", vS, pre_vS if sblk == 0 else vS.v)
            mm_done, evac_done, conv_done, silu_done = {}, {}, {}, {}
            w1fill = {}
            widths = (512, 512, 1)
            offs = (0, 512, 1024)
            P.wait("sync", tS, w1_guard[0])
            for g in (0, 1):
                w1fill[g] = P.dma("sync", A[:, A_W1S + 2048 * (g % 2):A_W1S + 2048 * (g % 2) + 2048],
                                  scr_w1[:, 2048 * g:2048 * g + 2048], q_w1[g % 2])
            for cj in range(NCJ):
                g, gi = cj // 2, cj % 2
                par = g % 2
                if gi == 0 and g >= 1 and g + 1 < 16:
                    P.wait("sync", tS, mm_done[2 * (g - 1) + 1])
                    w1fill[g + 1] = P.dma(
                        "sync", A[:, A_W1S + 2048 * ((g + 1) % 2):A_W1S + 2048 * ((g + 1) % 2) + 2048],
                        scr_w1[:, 2048 * (g + 1):2048 * (g + 1) + 2048], q_w1[(g + 1) % 2])
                w1sl = A[:, A_W1S + 2048 * par + 1024 * gi:A_W1S + 2048 * par + 1024 * gi + 1024]
                PS = PS_A if cj % 2 == 0 else PS_B
                subs = (PS[:, 0:512], PS[:, 512:1024], PS[:, 1024:1025])
                P.wait("tensor", q_w1[par], w1fill[g])
                P.wait("tensor", sS, evac_done.get(cj - 2, 0))
                last = None
                for k in range(8):
                    stat = w1sl[:, 128 * k:128 * k + 128]
                    for si in range(3):
                        mv_ap = A[:, A_N1T + 2048 * k + win_lo + offs[si]:
                                  A_N1T + 2048 * k + win_lo + offs[si] + widths[si]]
                        last = P.do("tensor", lambda t_, sub=subs[si], stat=stat, mv=mv_ap,
                                    st=(k == 0), sp=(k == 7): t_.matmul(
                            sub, stat, mv, start=st, stop=sp),
                            inc=(k == 7 and si == 2))
                mm_done[cj] = last
                if cj == NCJ - 1:
                    w1_guard[0] = last
                hb = F_HRAW[cj % 2]
                P.wait("scalar", tS, last)
                P.wait("scalar", vS, conv_done.get(cj - 2, 0))
                ed = P.do("scalar", lambda s, PS=PS, hb=hb, o=evac_off: s.activation(
                    F[:, hb + o:hb + o + 1025], PS[:, 0:1025], AF.Identity))
                evac_done[cj] = ed
                ct = F[:, F_CT:F_CT + 512 * (cj % 2) + 0:] if False else \
                     (F[:, F_CT:F_CT + 1024] if cj % 2 == 0 else F[:, F_XQT:F_XQT + 1024])
                P.wait("vector", sS, ed)
                P.do("vector", lambda v, hb=hb, cj=cj, ct=ct: v.tensor_scalar(
                    ct, F[:, hb:hb + 1024], cwp[:, cj:cj + 1], None, OP.mult), inc=False)
                P.do("vector", lambda v, hb=hb, cj=cj, ct=ct: v.scalar_tensor_tensor(
                    ct, F[:, hb + 1:hb + 1025], cwp[:, 32 + cj:32 + cj + 1], ct,
                    OP.mult, OP.add), inc=False)
                conv_done[cj] = P.do("vector", lambda v, hb=hb, cj=cj, ct=ct: v.scalar_tensor_tensor(
                    ct, F[:, hb + 2:hb + 1026], cwp[:, 64 + cj:64 + cj + 1], ct,
                    OP.mult, OP.add))
                def emit_silu(pj):
                    P.wait("scalar", vS, conv_done[pj])
                    hs = A[:, A_HS + 1024 * pj:A_HS + 1024 * pj + 1024]
                    cts = F[:, F_CT:F_CT + 1024] if pj % 2 == 0 else F[:, F_XQT:F_XQT + 1024]
                    silu_done[pj] = P.do("scalar", lambda s, hs=hs, pj=pj, cts=cts: s.activation(
                        hs, cts, AF.Silu, bias=cbp[:, pj:pj + 1]))
                if cj >= 1:
                    emit_silu(cj - 1)
                if cj >= 2:
                    pj = cj - 2
                    P.wait("vector", sS, silu_done[pj])
                    if pj == 0:
                        P.do("vector", lambda v: v.tensor_copy(
                            A[:, A_M:A_M + 1024], A[:, A_HS:A_HS + 1024]))
                    else:
                        P.do("vector", lambda v, pj=pj: v.tensor_tensor(
                            A[:, A_M:A_M + 1024], A[:, A_M:A_M + 1024],
                            A[:, A_HS + 1024 * pj:A_HS + 1024 * pj + 1024], OP.max))
            emit_silu(NCJ - 1)
            for pj in (NCJ - 2, NCJ - 1):
                P.wait("vector", sS, silu_done[pj])
                P.do("vector", lambda v, pj=pj: v.tensor_tensor(
                    A[:, A_M:A_M + 1024], A[:, A_M:A_M + 1024],
                    A[:, A_HS + 1024 * pj:A_HS + 1024 * pj + 1024], OP.max))
            P.selfwait("vector")
            if ffn_stop == "hsilu":
                P.barrier()
                continue
            P.do("vector", lambda v: v.tensor_copy(F[:, F_CT:F_CT + 1024], A[:, A_M:A_M + 1024]))
            P.selfwait("vector")
            P.wait("tensor", vS, vS.v)
            P.wait("tensor", sS, sS.v)
            tl = None
            for k in range(8):
                tl = P.do("tensor", lambda t_, k=k: t_.transpose(
                    PS_C[:, 128 * k:128 * k + 128], F[:, F_CT + 128 * k:F_CT + 128 * k + 128],
                    idf[:, :]), inc=(k == 7))
            P.wait("vector", tS, tl)
            for k in range(8):
                P.do("vector", lambda v, k=k: v.tensor_reduce(
                    F[:, F_MX2 + k:F_MX2 + k + 1], PS_C[:, 128 * k:128 * k + 128], AX.X, OP.max),
                    inc=(k == 7))
            P.selfwait("vector")
            dov(lambda v: v.tensor_scalar_max(F[:, F_MX2:F_MX2 + 8], F[:, F_MX2:F_MX2 + 8], 1e-5))
            dov(lambda v: v.reciprocal(F[:, F_S2:F_S2 + 8], F[:, F_MX2:F_MX2 + 8]))
            dov(lambda v: v.tensor_scalar(F[:, F_S2:F_S2 + 8], F[:, F_S2:F_S2 + 8],
                                          127.0, None, OP.mult))
            dov(lambda v: v.tensor_copy(A[0:128, A_S2BC:A_S2BC + 8], F[:, F_S2:F_S2 + 8]))
            dov(lambda v: v.tensor_copy(F[:, F_S2P:F_S2P + 8], A[0:128, A_S2BC:A_S2BC + 8]))
            dov(lambda v: v.reciprocal(F[:, F_ESC:F_ESC + 8], F[:, F_S2P:F_S2P + 8]))
            dov(lambda v: v.tensor_scalar(F[:, F_ESC:F_ESC + 8], F[:, F_ESC:F_ESC + 8],
                                          mwb[:, 1:2], None, OP.mult))
            P.wait("tensor", vS, vS.v)
            tl = None
            for k in range(8):
                tl = P.do("tensor", lambda t_, k=k: t_.transpose(
                    PS_A[0:1, 128 * k:128 * k + 128], F[:, F_S2P + k:F_S2P + k + 1],
                    idf[:, :]), inc=(k == 7))
            P.wait("scalar", tS, tl)
            P.do("scalar", lambda s: s.activation(A[0:1, A_S2R:A_S2R + 1024], PS_A[0:1, 0:1024],
                                                  AF.Identity))
            P.wait("tensor", sS, sS.v)
            tl = None
            for half in range(2):
                tl = P.do("tensor", lambda t_, half=half: t_.matmul(
                    PS_B[:, 512 * half:512 * half + 512], onesb[0:1, :],
                    A[0:1, A_S2R + 512 * half:A_S2R + 512 * half + 512],
                    start=True, stop=True), inc=(half == 1))
            P.wait("scalar", tS, tl)
            P.do("scalar", lambda s: s.activation(A[:, A_S2BC:A_S2BC + 1024], PS_B[:, 0:1024],
                                                  AF.Identity))
            P.wait("vector", sS, sS.v)
            P.wait("tensor", vS, vS.v)
            P.wait("tensor", sS, sS.v)
            n2done = {}
            oct_ps = [PS_A[:, 0:512], PS_A[:, 512:1024], PS_A[:, 1024:1536],
                      PS_B[:, 0:512], PS_B[:, 512:1024], PS_B[:, 1024:1536],
                      PS_C[:, 0:512], PS_C[:, 512:1024]]
            slot_guard = {0: w2_guard[0], 1: w2_guard[0]}
            for h2 in range(2):
                fills = {}
                for g in (0, 1):
                    P.wait("sync", tS, slot_guard[g % 2])
                    fills[g] = P.dma(
                        "sync", A[:, A_W2S + 2048 * (g % 2):A_W2S + 2048 * (g % 2) + 2048],
                        scr_w2[:, 16384 * h2 + 2048 * g:16384 * h2 + 2048 * g + 2048],
                        q_w2[g % 2])
                grp_last = {}
                for cj in range(NCJ):
                    g, gi = cj // 4, cj % 4
                    par = g % 2
                    if gi == 0 and g >= 1 and g + 1 < 8:
                        P.wait("sync", tS, grp_last[g - 1])
                        fills[g + 1] = P.dma(
                            "sync",
                            A[:, A_W2S + 2048 * ((g + 1) % 2):A_W2S + 2048 * ((g + 1) % 2) + 2048],
                            scr_w2[:, 16384 * h2 + 2048 * (g + 1):16384 * h2 + 2048 * (g + 1) + 2048],
                            q_w2[(g + 1) % 2])
                    if h2 == 0:
                        for h in range(2):
                            if SIM_SERIAL:
                                P.do("vector", lambda v, cj=cj, h=h: v.tensor_tensor(
                                    F[:, F_HRAW[0]:F_HRAW[0] + 512],
                                    A[:, A_HS + 1024 * cj + 512 * h:A_HS + 1024 * cj + 512 * h + 512],
                                    A[:, A_S2BC + 512 * h:A_S2BC + 512 * h + 512], OP.mult))
                                rap = emit_round(F[:, F_HRAW[0]:F_HRAW[0] + 512])
                            else:
                                P.do("vector", lambda v, cj=cj, h=h: v.tensor_tensor(
                                    sint[:, :],
                                    A[:, A_HS + 1024 * cj + 512 * h:A_HS + 1024 * cj + 512 * h + 512],
                                    A[:, A_S2BC + 512 * h:A_S2BC + 512 * h + 512], OP.mult),
                                    inc=False)
                                rap = sint[:, :]
                            n2done[cj] = P.do("vector", lambda v, cj=cj, h=h, rap=rap: v.tensor_copy(
                                A[:, A_HS + 1024 * cj + 512 * h:A_HS + 1024 * cj + 512 * h + 512],
                                rap))
                        P.wait("tensor", vS, n2done[cj])
                    w2sl = A[:, A_W2S + 2048 * par + 512 * gi:A_W2S + 2048 * par + 512 * gi + 512]
                    P.wait("tensor", q_w2[par], fills[g])
                    last = None
                    for tt in range(8):
                        pp = oct_ps[tt]
                        last = P.do("tensor", lambda t_, cj=cj, tt=tt, pp=pp, w2sl=w2sl: t_.matmul(
                            pp, A[:, A_HS + 1024 * cj + 128 * tt:A_HS + 1024 * cj + 128 * tt + 128],
                            w2sl, start=(cj == 0), stop=(cj == NCJ - 1)), inc=(tt == 7))
                    if gi == 3:
                        grp_last[g] = last
                slot_guard[0] = slot_guard[1] = last
                w2_guard[0] = last
                P.wait("vector", tS, last)
                for tt in range(8):
                    c_glob = 8 * sblk + tt
                    pp = oct_ps[tt]
                    xsl = x_sb[:, 1024 * c_glob + 512 * h2:1024 * c_glob + 512 * h2 + 512]
                    P.do("vector", lambda v, pp=pp, xsl=xsl, tt=tt: v.scalar_tensor_tensor(
                        xsl, pp, F[:, F_ESC + tt:F_ESC + tt + 1], xsl, OP.mult, OP.add),
                        inc=(tt == 7))
                P.wait("tensor", vS, vS.v)
            marks.append(vS.v)
            if sblk == 0 and mid_hook is not None:
                mid_hook()
        return marks

    def attn_stage(seq, pre_done=0):
        P.wait("sync", tS, tS.v)
        P.dma("sync", A[:, 37120:37120 + 8192], scr_wv[:, :], q_wv)
        ln_stats(for_ffn=False, pre_done=pre_done)
        evacd = {}
        on_done = {}
        sdone_holder = {}

        def emit_v(tt):
            pv = PS_B
            P.wait("tensor", q_wv, q_wv.v)
            P.wait("tensor", sS, max(evacd.get(tt - 1, 0), sdone_holder[tt]))
            last = None
            for k in range(8):
                for half in range(2):
                    P.do("tensor", lambda t_, k=k, half=half, tt=tt, pv=pv: t_.matmul(
                        pv[:, 512 * half:512 * half + 512],
                        A[:, A_N1T + 2048 * k + 128 * tt:A_N1T + 2048 * k + 128 * tt + 128],
                        A[:, 37120 + 1024 * k + 512 * half:37120 + 1024 * k + 512 * half + 512],
                        start=(k == 0), stop=False), inc=False)
            for half in range(2):
                last = P.do("tensor", lambda t_, half=half, pv=pv: t_.matmul(
                    pv[:, 512 * half:512 * half + 512], onesb[0:1, :],
                    bvb[0:1, 512 * half:512 * half + 512], start=False, stop=True),
                    inc=(half == 1))
            P.wait("scalar", tS, last)
            P.wait("scalar", vS, on_done.get(tt - 1, 0))
            vg = r2(A, AT_VG + 1040 * tt, 1040, 65)
            evacd[tt] = P.do("scalar", lambda s, vg=vg, pv=pv: s.activation(
                vg[:, :, 0:64], r2(pv, 0, 1024, 64), AF.Identity))
            P.wait("vector", sS, evacd[tt])
            on_done[tt] = P.do("vector", lambda v, vg=vg: v.memset(vg[:, :, 64:65], 1.0))

        def chase_v(c):
            sdone_holder.update(xh_partial())
            if c >= 1:
                emit_v(c - 1)
            if c == NCH - 1:
                emit_v(c)

        xh_ref = [None]

        def xh_partial():
            return xh_ref[0] if xh_ref[0] is not None else {}

        xh_ref[0] = sdone_cap = {}
        def hook(c):
            sdone_cap.update({c: None})
            chase_v(c)

        xh_done = make_xhatT(for_ffn=False, interleave=None, v_hook=emit_v,
                             v_sdone=sdone_holder)
        P.wait("tensor", sS, sS.v)
        P.wait("tensor", vS, vS.v)
        attn_pend = [None]
        prev_ce = [0]
        prev_expd = [{}]
        qkfill = {}
        P.wait("sync", tS, tS.v)
        for hp in (0, 1):
            par = hp % 2
            P.dma("sync", A[:, AT_WQS + 2048 * par:AT_WQS + 2048 * par + 1024],
                  scr_wq[:, 1024 * hp:1024 * hp + 1024], q_qk[par])
            qkfill[hp] = P.dma("sync", A[:, AT_WQS + 2048 * par + 1024:AT_WQS + 2048 * par + 2048],
                               scr_wk[:, 1024 * hp:1024 * hp + 1024], q_qk[par])
        for hp in range(8):
            par = hp % 2
            P.wait("tensor", q_qk[par], qkfill[hp])
            P.wait("tensor", vS, vS.v)
            pevac = {}
            for which, dstc, scale_t, bias_t in ((0, AT_Q, scq, bqs), (1, AT_K, sck, bks)):
                wsl = A[:, AT_WQS + 2048 * par + 1024 * which:
                        AT_WQS + 2048 * par + 1024 * which + 1024]
                for b in range(4):
                    g_lin = 4 * which + b
                    ph = PS_C[:, 512 * (g_lin % 2):512 * (g_lin % 2) + 512]
                    P.wait("tensor", sS, pevac.get(g_lin - 2, 0))
                    last = None
                    for k in range(8):
                        last = P.do("tensor", lambda t_, k=k, b=b, ph=ph, wsl=wsl: t_.matmul(
                            ph, wsl[:, 128 * k:128 * k + 128],
                            A[:, A_N1T + 2048 * k + 512 * b:A_N1T + 2048 * k + 512 * b + 512],
                            start=(k == 0), stop=(k == 7)), inc=(k == 7))
                    P.wait("scalar", tS, last)
                    pevac[g_lin] = P.do("scalar", lambda s, ph=ph, b=b, dstc=dstc,
                                             scale_t=scale_t, bias_t=bias_t, hp=hp: s.activation(
                        A[:, dstc + 512 * b:dstc + 512 * b + 512], ph, AF.Identity,
                        bias=bias_t[:, hp:hp + 1], scale=scale_t[:, hp:hp + 1]))
            if hp + 2 < 8:
                P.wait("sync", tS, tS.v)
                P.dma("sync", A[:, AT_WQS + 2048 * par:AT_WQS + 2048 * par + 1024],
                      scr_wq[:, 1024 * (hp + 2):1024 * (hp + 2) + 1024], q_qk[par])
                qkfill[hp + 2] = P.dma(
                    "sync", A[:, AT_WQS + 2048 * par + 1024:AT_WQS + 2048 * par + 2048],
                    scr_wk[:, 1024 * (hp + 2):1024 * (hp + 2) + 1024], q_qk[par])
            P.wait("tensor", sS, sS.v)
            pend = attn_pend[0]

            def ctx_ps(hi):
                return (PS_A if hi == 0 else PS_B)[0:65, 1024:1536]

            def emit_finalize():
                ph, pb, ce1 = attn_pend[0]
                attn_pend[0] = None
                P.wait("tensor", sS, sS.v)
                P.wait("tensor", vS, max(vS.v if False else ce1, ce1))
                P.wait("tensor", vS, vS.v)
                tl = None
                for hi in range(2):
                    base = F_CTXA if hi == 0 else F_CTXS
                    pdst = 0 if hi == 0 else 512
                    for g in range(4):
                        tl = P.do("tensor", lambda t_, g=g, base=base, pdst=pdst: t_.transpose(
                            PS_C[:, pdst + 65 * g:pdst + 65 * g + 65],
                            F[0:65, base + 128 * g:base + 128 * g + 128],
                            idf[0:65, 0:65]), inc=(hi == 1 and g == 3))
                P.wait("vector", tS, tl)
                for hi in range(2):
                    pdst = 0 if hi == 0 else 512
                    sums = bass.AP(PS_C, pdst + 64, [[1024, 128], [65, 4]])
                    P.do("vector", lambda v, sums=sums, hi=hi: v.reciprocal(
                        F[:, F_RV + 4 * hi:F_RV + 4 * hi + 4], sums), inc=(hi == 1))
                P.selfwait("vector")
                for hi in range(2):
                    pdst = 0 if hi == 0 else 512
                    for g in range(4):
                        c_glob = 4 * pb + g
                        xsl = x_sb[:, 1024 * c_glob + 64 * (2 * ph + hi):
                                   1024 * c_glob + 64 * (2 * ph + hi) + 64]
                        P.do("vector", lambda v, g=g, xsl=xsl, pdst=pdst, hi=hi:
                             v.scalar_tensor_tensor(
                                 xsl, PS_C[:, pdst + 65 * g:pdst + 65 * g + 64],
                                 F[:, F_RV + 4 * hi + g:F_RV + 4 * hi + g + 1],
                                 xsl, OP.mult, OP.add), inc=(hi == 1 and g == 3))
                return vS.v

            for blk in range(4):
                expd = dict(prev_expd[0])
                ctx_last = {}
                for tk in range(NCH):
                    if tk == 2 and attn_pend[0] is not None:
                        emit_finalize()
                    pp = PS_A if tk % 2 == 0 else PS_B
                    P.wait("tensor", sS, expd.get(tk - 2, 0))
                    P.do("tensor", lambda t_, tk=tk, blk=blk, pp=pp: t_.matmul(
                        pp[:, 0:512], A[0:64, AT_K + 128 * tk:AT_K + 128 * tk + 128],
                        A[0:64, AT_Q + 512 * blk:AT_Q + 512 * blk + 512],
                        start=True, stop=True, tile_position=(0, 0)), inc=False)
                    m1 = P.do("tensor", lambda t_, tk=tk, blk=blk, pp=pp: t_.matmul(
                        pp[:, 512:1024], A[64:128, AT_K + 128 * tk:AT_K + 128 * tk + 128],
                        A[64:128, AT_Q + 512 * blk:AT_Q + 512 * blk + 512],
                        start=True, stop=True, tile_position=(64, 0)))
                    if tk >= 1:
                        if tk == 1:
                            P.wait("tensor", vS, prev_ce[0])
                        ptp = A[:, PT_SLOT[(tk - 1) % 4]:PT_SLOT[(tk - 1) % 4] + 1024]
                        P.wait("tensor", sS, expd[tk - 1])
                        for hi in range(2):
                            vg = A[:, AT_VG + 1040 * (tk - 1) + 65 * (2 * hp + hi):
                                   AT_VG + 1040 * (tk - 1) + 65 * (2 * hp + hi) + 65]
                            ctx_last[tk - 1] = P.do("tensor", lambda t_, vg=vg, hi=hi, ptp=ptp,
                                                    pt_=tk - 1: t_.matmul(
                                ctx_ps(hi), vg, ptp[:, 512 * hi:512 * hi + 512],
                                start=(pt_ == 0), stop=(pt_ == NCH - 1)), inc=(hi == 1))
                    pt = A[:, PT_SLOT[tk % 4]:PT_SLOT[tk % 4] + 1024]
                    P.wait("scalar", tS, m1)
                    if tk >= 4:
                        P.wait("scalar", tS, ctx_last[tk - 4])
                    expd[tk] = P.do("scalar", lambda s, pp=pp, pt=pt: s.activation(
                        pt, pp[:, 0:1024], AF.Exp, scale=0.125))
                ptp = A[:, PT_SLOT[(NCH - 1) % 4]:PT_SLOT[(NCH - 1) % 4] + 1024]
                P.wait("tensor", sS, expd[NCH - 1])
                for hi in range(2):
                    vg = A[:, AT_VG + 1040 * (NCH - 1) + 65 * (2 * hp + hi):
                           AT_VG + 1040 * (NCH - 1) + 65 * (2 * hp + hi) + 65]
                    ctx_last[NCH - 1] = P.do("tensor", lambda t_, vg=vg, hi=hi, ptp=ptp: t_.matmul(
                        ctx_ps(hi), vg, ptp[:, 512 * hi:512 * hi + 512],
                        start=False, stop=True), inc=(hi == 1))
                ce1 = None
                for hi in range(2):
                    P.wait("vector", tS, ctx_last[NCH - 1])
                    base = F_CTXA if hi == 0 else F_CTXS
                    ce1 = P.do("vector", lambda v, hi=hi, base=base: v.tensor_copy(
                        F[0:65, base:base + 512], ctx_ps(hi)))
                attn_pend[0] = (hp, blk, ce1)
                prev_ce[0] = ce1
                prev_expd[0] = {14 - 16: expd[14], 15 - 16: expd[15]}
        if attn_pend[0] is not None:
            emit_finalize()
        w1_guard[0] = max(w1_guard[0], tS.v)

    steps = [(0, "A"), (0, "T"), (0, "C"), (1, "A"), (1, "T"), (1, "C")][:n_steps]
    loaded = set()
    if n_steps == 0:
        src0 = bass.AP(xin, 0, [[D, 128], [128 * D, NCH], [1, D]])
        P.dma("gpsimd", bass.AP(x_sb, 0, [[NCH * D, 128], [D, NCH], [1, D]]), src0, q_x[0])
        for e in ENGS:
            P.wait(e, q_x[0], q_x[0].v)
        loaded.add(0)
    for si, (seq, st) in enumerate(steps):
        if seq not in loaded:
            if seq == 1:
                for e in ENGS:
                    P.wait(e, q_out[0], q_out[0].v)
            if seq == 0:
                P.wait("gpsimd", vS, vS.v)
            src = bass.AP(xin, seq * S * D, [[D, 128], [128 * D, NCH], [1, D]])
            P.dma("gpsimd", bass.AP(x_sb, 0, [[NCH * D, 128], [D, NCH], [1, D]]), src, q_x[seq])
            for e in ENGS:
                P.wait(e, q_x[seq], q_x[seq].v)
            loaded.add(seq)
        marks = None
        if st in ("A", "C"):
            nxt = steps[si + 1][1] if si + 1 < len(steps) else None
            hook = None
            if nxt == "T":
                def hook():
                    ln_stats_chunks(False, 0, 8, F_MX)
            marks = ffn_stage(seq, mid_hook=hook)
        else:
            attn_stage(seq, pre_done=0 if steps[si - 1][1] not in ("A",) else 8)
        last_of_seq = (si == len(steps) - 1) or (steps[si + 1][0] != seq)
        if last_of_seq:
            if marks is not None and len(marks) == 2:
                for half, mk in enumerate(marks):
                    P.wait("gpsimd", vS, mk)
                    dst = bass.AP(xout, seq * S * D + half * 8 * 128 * D,
                                  [[D, 128], [128 * D, 8], [1, D]])
                    srcp = bass.AP(x_sb, half * 8 * D, [[NCH * D, 128], [D, 8], [1, D]])
                    P.dma("gpsimd", dst, srcp, q_out[seq])
            else:
                P.barrier()
                dst = bass.AP(xout, seq * S * D, [[D, 128], [128 * D, NCH], [1, D]])
                P.dma("gpsimd", dst, bass.AP(x_sb, 0, [[NCH * D, 128], [D, NCH], [1, D]]),
                      q_out[seq])
    if dump is not None or n_steps == 0:
        P.barrier()
        for s_ in (q_out[0], q_out[1]):
            P.wait("gpsimd", s_, s_.v)
        if dump is None:
            dst = bass.AP(xout, 0, [[D, 128], [128 * D, NCH], [1, D]])
            P.dma("gpsimd", dst, bass.AP(x_sb, 0, [[NCH * D, 128], [D, NCH], [1, D]]), q_out[0])
        else:
            which, lo, ncols = dump
            tsrc = {"A": A, "F": F, "X": x_sb}[which]
            dst = bass.AP(xout, 0, [[ncols, 128], [1, ncols]])
            P.dma("gpsimd", dst, tsrc[:, lo:lo + ncols], q_out[0])
    for seq in (0, 1):
        if q_out[seq].v:
            P.wait("gpsimd", q_out[seq], q_out[seq].v)

    with nc.Block() as block:
        @block.sync
        def _(eng):
            for fn in P.ops["sync"]:
                fn(eng)

        @block.tensor
        def _(eng):
            for fn in P.ops["tensor"]:
                fn(eng)

        @block.vector
        def _(eng):
            for fn in P.ops["vector"]:
                fn(eng)

        @block.scalar
        def _(eng):
            for fn in P.ops["scalar"]:
                fn(eng)

        @block.gpsimd
        def _(eng):
            for fn in P.ops["gpsimd"]:
                fn(eng)

    es.close()
    return nc


def kernel(L0_attn_hidden_states, L0_attn_Wq, L0_attn_bq, L0_attn_Wk, L0_attn_bk,
           L0_attn_Wv, L0_attn_bv, L0_ffn_w1, L0_ffn_conv_w, L0_ffn_conv_b, L0_ffn_w2,
           _n_steps=6):
    x = np.ascontiguousarray(L0_attn_hidden_states, dtype=np.float32)
    nc = build_program(_n_steps)
    cf = lambda a: np.ascontiguousarray(a, dtype=np.float32)
    in_maps = []
    for c in range(8):
        in_maps.append({
            "xin": cf(x[2 * c:2 * c + 2]),
            "wq": cf(L0_attn_Wq), "wk": cf(L0_attn_Wk), "wv": cf(L0_attn_Wv),
            "bq": cf(L0_attn_bq), "bk": cf(L0_attn_bk), "bv": cf(L0_attn_bv),
            "w1": cf(L0_ffn_w1), "w2": cf(L0_ffn_w2),
            "cw": cf(L0_ffn_conv_w), "cb": cf(L0_ffn_conv_b),
        })
    res = run_bass_kernel_spmd(nc, in_maps, core_ids=list(range(8)))
    out = np.concatenate([r["xout"] for r in res.results], axis=0)
    return out
